# version 09
# baseline (speedup 1.0000x reference)
import numpy as np
import ml_dtypes
from contextlib import ExitStack

import concourse.bass as bass
import concourse.mybir as mybir
from concourse.bass_utils import run_bass_kernel_spmd

F32 = mybir.dt.float32
BF16 = mybir.dt.bfloat16
I32 = mybir.dt.int32
AF = mybir.ActivationFunctionType
ALU = mybir.AluOpType
AX = mybir.AxisListType

S = 2048
DM = 2048
H = 16
HD = 128
NT = 16
NCH = 16
DFF = 8192
NICH = 64
NSEQ = 2
NCORE = 8
QSCALE = float(HD) ** -0.5
EPS = 1e-5
NEGB = -3.0e38

ENGINES = ("sync", "tensor", "vector", "scalar", "gpsimd")
QSEM = {"tensor": "qT", "vector": "qV", "scalar": "qA", "gpsimd": "qG"}

BF = ml_dtypes.bfloat16


class Plan:
    def __init__(self, nc, ctx):
        self.nc = nc
        self.ctx = ctx
        self.ops = {e: [] for e in ENGINES}
        self._sems = {}
        self.nphase = 0

    def sem(self, name):
        if name not in self._sems:
            h = self.ctx.enter_context(self.nc.semaphore(name))
            self._sems[name] = [h, 0]
        return name

    def cnt(self, name):
        self.sem(name)
        return self._sems[name][1]

    def emit(self, eng, fn=None, wait=(), sig=None, dstep=16):
        wl = []
        for n, v in wait:
            self.sem(n)
            if v > 0:
                wl.append((self._sems[n][0], v))
        mark = None
        il = None
        if sig is not None:
            self.sem(sig)
            d = dstep if sig.startswith("d") else 1
            self._sems[sig][1] += d
            il = (self._sems[sig][0], d)
            mark = (sig, self._sems[sig][1])

        def op(e, wl=wl, il=il, fn=fn):
            for h, v in wl:
                e.wait_ge(h, v)
            ins = e.nop() if fn is None else fn(e)
            if il is not None:
                ins.then_inc(il[0], il[1])

        self.ops[eng].append(op)
        return mark

    def barrier(self):
        self.nphase += 1
        sems = ("qT", "qV", "qA", "qG",
                "dst0", "dst1", "dstf", "dsq0", "dsq1", "dsv0", "dsv1",
                "dso0", "dso1", "dstt0", "dstt1", "dpo0", "dpo1", "dpl0", "dpl1")
        w = [(q, self.cnt(q)) for q in sems if self.cnt(q) > 0]
        for e in ENGINES:
            self.emit(e, None, wait=w)


class Ring:

    def __init__(self, plan, name, n):
        self.p = plan
        self.n = n
        self.name = name
        self.marks = []
        self.rel = []
        self.ip = 0
        self.ic = 0

    def pw(self):
        i = self.ip
        if i < self.n:
            return []
        return [self.rel[i - self.n]]

    def pd(self, mark):
        assert mark is not None, f"ring {self.name}: producer op must signal"
        self.marks.append(mark)
        self.ip += 1

    def cw(self):
        i = self.ic
        self.ic += 1
        return [self.marks[i]]

    def release(self, mark):
        assert mark is not None, f"ring {self.name}: consumer op must signal"
        self.rel.append(mark)

    def psl(self):
        return self.ip % self.n

    def csl(self):
        return (self.ic - 1) % self.n


class Bufs:
    def __init__(self, nc, ctx, plan):
        self.nc = nc
        AR_F32 = 48 * 1024
        self.ar_f = ctx.enter_context(nc.sbuf_tensor("arena", [128, AR_F32], F32))
        self.ar_b = self.ar_f.bitcast(BF16)
        self.ar_i = self.ar_f.bitcast(I32)

        T = lambda n, s, d: ctx.enter_context(nc.sbuf_tensor(n, s, d))
        self.identbf = T("identbf", [128, 128], BF16)
        self.identf = T("identf", [128, 128], F32)
        self.onesbf = T("onesbf", [1, 128], BF16)
        self.onesf = T("onesf", [1, 128], F32)
        self.masks = T("masks", [128, 4 * 512], BF16)
        self.biasq = T("biasq", [128, 96], F32)
        self.brow = T("brow", [1, 2 * DM], BF16)
        self.stats = T("stats", [128, 64], F32)
        self.scT = T("scT", [128, 224], F32)
        self.osb = T("osb", [128, 1], F32)
        self.tiny = T("tiny", [128, 16], F32)

        self.pa = ctx.enter_context(nc.psum_tensor("pa", [128, 2048], F32))
        self.pb = ctx.enter_context(nc.psum_tensor("pb", [128, 512], F32))
        self.pc = ctx.enter_context(nc.psum_tensor("pc", [128, 512], F32))
        self.pd = ctx.enter_context(nc.psum_tensor("pd", [128, 512], F32))
        self.pe = ctx.enter_context(nc.psum_tensor("pe", [128, 512], F32))
        self.acc6 = [self.pa[:, 512 * j:512 * (j + 1)] for j in range(4)] + [self.pb[:], self.pc[:]]
        self.tp2 = [self.pd[:], self.pe[:]]

    def bfv(self, byte_off, nelem):
        assert byte_off % 2 == 0
        o = byte_off // 2
        return self.ar_b[:, o:o + nelem]

    def f32v(self, byte_off, nelem):
        assert byte_off % 4 == 0
        o = byte_off // 4
        return self.ar_f[:, o:o + nelem]

    def i32v(self, byte_off, nelem):
        o = byte_off // 4
        return self.ar_i[:, o:o + nelem]

    HFM = 0
    KFM = 65536
    WORK = 131072

    def hfm3(self):
        return self.bfv(self.HFM, 32768).rearrange("p (c t) -> p c t", t=S)

    def kfm3(self):
        return self.bfv(self.KFM, 32768).rearrange("p (c t) -> p c t", t=S)

    def vtm(self):
        return self.bfv(self.HFM, 32768).rearrange("p (k h d) -> p k h d", h=H, d=HD)

    def treg(self):
        return self.bfv(self.HFM, 32768).rearrange("p (i t) -> p i t", t=512)

    def bnb(self, sl):
        return self.stats[:, 24 * sl:24 * sl + 24]

    def mv(self, sl):
        return self.stats[:, 48 + 2 * sl:48 + 2 * sl + 2]

    def v1(self, sl):
        return self.stats[:, 52 + sl:53 + sl]

    def std(self, sl):
        return self.stats[:, 54 + sl:55 + sl]

    def rr(self, sl):
        return self.stats[:, 56 + sl:57 + sl]

    def gmax(self):
        return self.stats[:, 60:64]

    def mask(self, tl):
        return self.masks[:, 512 * tl:512 * (tl + 1)]


def emit_stats_a(P, B, srcs4, sl, wait0=()):
    m = None
    for g, sr in enumerate(srcs4):
        m = P.emit("vector", lambda e, sr=sr, g=g, sl=sl: e.bn_stats(
            out=B.bnb(sl)[:, 6 * g:6 * g + 6], in_=sr),
            wait=(list(wait0) if g == 0 else []), sig=("qV" if g == 3 else None))
    m = P.emit("vector", lambda e, sl=sl: e.bn_aggr(out=B.mv(sl), in_=B.bnb(sl)),
               wait=[m], sig="qV")
    m = P.emit("vector", lambda e, sl=sl: e.tensor_scalar(
        out=B.v1(sl), in0=B.mv(sl)[:, 1:2], scalar1=EPS, scalar2=None, op0=ALU.add),
               wait=[m], sig="qV")
    return m


def emit_stats_b(P, B, sl, ma):
    m = P.emit("scalar", lambda e, sl=sl: e.activation(
        out=B.std(sl), in_=B.v1(sl), func=AF.Sqrt, bias=0.0, scale=1.0),
               wait=[ma], sig="qA")
    m = P.emit("vector", lambda e, sl=sl: e.reciprocal(out=B.rr(sl), in_=B.std(sl)),
               wait=[m], sig="qV")
    return m


def emit_stats(P, B, srcs4, sl, wait0=()):
    m = None
    for g, s in enumerate(srcs4):
        m = P.emit("vector", lambda e, s=s, g=g, sl=sl: e.bn_stats(
            out=B.bnb(sl)[:, 6 * g:6 * g + 6], in_=s),
            wait=(list(wait0) if g == 0 else []), sig=("qV" if g == 3 else None))
    m = P.emit("vector", lambda e, sl=sl: e.bn_aggr(out=B.mv(sl), in_=B.bnb(sl)),
               wait=[m], sig="qV")
    m = P.emit("vector", lambda e, sl=sl: e.tensor_scalar(
        out=B.v1(sl), in0=B.mv(sl)[:, 1:2], scalar1=EPS, scalar2=None, op0=ALU.add),
               wait=[m], sig="qV")
    m = P.emit("scalar", lambda e, sl=sl: e.activation(
        out=B.std(sl), in_=B.v1(sl), func=AF.Sqrt, bias=0.0, scale=1.0),
               wait=[m], sig="qA")
    m = P.emit("vector", lambda e, sl=sl: e.reciprocal(out=B.rr(sl), in_=B.std(sl)),
               wait=[m], sig="qV")
    return m


def emit_p1(P, B, T, src_seq_ap, wait_first=()):
    W = B.WORK
    xts = [B.f32v(W, 2048), B.f32v(W + 8192, 2048)]
    hts = [B.bfv(W + 16384, 2048), B.bfv(W + 20480, 2048)]
    rx = B.r_p1x
    rh = B.r_p1h
    hfm = B.hfm3()
    pend = None
    slot_rel = {0: [], 1: []}

    def finish(tt, ssl, xap, ma):
        rw = emit_stats_b(P, B, ssl, ma)
        mn = P.emit("vector", lambda e, ssl=ssl: e.tensor_scalar(
            out=B.stats[:, 58 + ssl:59 + ssl], in0=B.mv(ssl)[:, 0:1],
            scalar1=B.rr(ssl), scalar2=-1.0, op0=ALU.mult, op1=ALU.mult),
               wait=[rw], sig="qV")
        hsl = rh.psl()
        mk = P.emit("scalar", lambda e, xap=xap, hsl=hsl, ssl=ssl: e.activation(
            out=hts[hsl][:], in_=xap, func=AF.Identity,
            bias=B.stats[:, 58 + ssl:59 + ssl], scale=B.rr(ssl)),
               wait=[mn] + rh.pw(), sig="qA")
        rh.pd(mk)
        rx.release(mk)
        cwh = rh.cw()
        hap = hts[rh.csl()]
        for pk in range(4):
            tpw = B.r_tp.pw()
            slot = B.r_tp.psl()
            mk = None
            for j in range(4):
                c = 4 * pk + j
                mk = P.emit("tensor", lambda e, c=c, j=j, slot=slot, hap=hap: e.matmul(
                    out=B.tp2[slot][:, 128 * j:128 * (j + 1)], lhsT=hap[:, 128 * c:128 * (c + 1)],
                    rhs=B.identbf[:], start=(j == 0), stop=(j == 3)),
                       wait=(cwh if pk == 0 and j == 0 else []) + (tpw if j == 0 else []),
                       sig=("qT" if j == 3 else None))
            B.r_tp.pd(mk)
            if pk == 3:
                rh.release(mk)
            mke = P.emit("scalar", lambda e, pk=pk, tt=tt, slot=slot, hfm=hfm: e.activation(
                out=hfm[:, 4 * pk:4 * pk + 4, 128 * tt:128 * (tt + 1)],
                in_=B.tp2[slot].rearrange("p (j t) -> p j t", j=4),
                func=AF.Identity, bias=0.0, scale=1.0),
                   wait=B.r_tp.cw(), sig="qA")
            B.r_tp.release(mke)
        slot_rel[ssl] = [mk, mn]

    for tt in range(NT):
        sl = rx.psl()
        mk = P.emit("sync", lambda e, t=tt, sl=sl, xts=xts, src_seq_ap=src_seq_ap: e.dma_start(
            out=xts[sl][:], in_=src_seq_ap[128 * t:128 * (t + 1), :]),
               wait=rx.pw() + (list(wait_first) if tt == 0 else []), sig=f"dx{sl}")
        rx.pd(mk)
        cwx = rx.cw()
        ssl = tt % 2
        xap = xts[rx.csl()]
        ma = emit_stats_a(P, B, [xap[:, 512 * g:512 * (g + 1)] for g in range(4)], ssl,
                          wait0=cwx + slot_rel[ssl])
        if pend is not None:
            finish(*pend)
        pend = (tt, ssl, xap, ma)
    finish(*pend)


def emit_qkv(P, B, T, layer, seq):
    W = B.WORK
    wsl = [B.bfv(W, 8192).rearrange("p (c o) -> p c o", c=16),
           B.bfv(W + 16384, 8192).rearrange("p (c o) -> p c o", c=16)]
    qst = [B.bfv(W + 32768, 2048), B.bfv(W + 36864, 2048)]
    vst = [B.bfv(W + 40960, 2048), B.bfv(W + 45056, 2048)]
    hfm = B.hfm3()
    kfm = B.kfm3()
    rw = B.r_w2
    rq = B.r_qst
    rv = B.r_vst
    n_win, opw = (12, 4) if layer == 0 else (48, 1)
    for w in range(n_win):
        slw = rw.psl()
        if layer == 0:
            src = T["s_qkvT"].ap()[w].rearrange("p (c o) -> p c o", o=512)
            mk = P.emit("sync", lambda e, src=src, slw=slw, wsl=wsl: e.dma_start(
                out=wsl[slw][:, :, 0:512], in_=src),
                   wait=rw.pw() + (T["prep_marks"]["s_qkvT"] if w == 0 else []),
                   sig=f"dw{slw}")
        else:
            src = T["s_l1qT"].ap()[w].rearrange("p (c o) -> p c o", o=128)
            mk = P.emit("sync", lambda e, src=src, slw=slw, wsl=wsl: e.dma_start(
                out=wsl[slw][:, :, 0:128], in_=src),
                   wait=rw.pw() + (T["prep_marks"]["s_l1qT"] if w == 0 else []),
                   sig=f"dw{slw}")
        rw.pd(mk)
        cww = rw.cw()
        first_mm_of_slab = True
        for ocl in range(opw):
            och = w * opw + ocl
            typ, hh = och // 16, och % 16
            for tokg in range(4):
                pw = B.r_ps6.pw()
                slot = B.r_ps6.psl()
                pap = B.acc6[slot]
                last_mm = None
                for cch in range(NCH):
                    last_mm = P.emit("tensor", lambda e, pap=pap, slw=slw, cch=cch, ocl=ocl, tokg=tokg, wsl=wsl, hfm=hfm: e.matmul(
                        out=pap, lhsT=wsl[slw][:, cch, 128 * ocl:128 * (ocl + 1)],
                        rhs=hfm[:, cch, 512 * tokg:512 * (tokg + 1)],
                        start=(cch == 0), stop=(cch == NCH - 1)),
                           wait=(cww if first_mm_of_slab else []) + (pw if cch == 0 else []),
                           sig=("qT" if cch == NCH - 1 else None))
                    first_mm_of_slab = False
                B.r_ps6.pd(last_mm)
                if ocl == opw - 1 and tokg == 3:
                    rw.release(last_mm)
                cwp = B.r_ps6.cw()
                bias_ap = B.biasq[:, 48 * layer + och:48 * layer + och + 1]
                if typ == 1:
                    mk = P.emit("vector", lambda e, pap=pap, hh=hh, tokg=tokg, bias_ap=bias_ap, kfm=kfm: e.tensor_scalar(
                        out=kfm[:, hh, 512 * tokg:512 * (tokg + 1)], in0=pap, scalar1=bias_ap,
                        scalar2=None, op0=ALU.add),
                           wait=cwp, sig="qV")
                    B.r_ps6.release(mk)
                else:
                    rr = rq if typ == 0 else rv
                    stg = qst if typ == 0 else vst
                    sls = rr.psl()
                    fn = lambda e, pap=pap, sls=sls, tokg=tokg, bias_ap=bias_ap, stg=stg: e.activation(
                        out=stg[sls][:, 512 * tokg:512 * (tokg + 1)], in_=pap, func=AF.Identity,
                        bias=bias_ap, scale=1.0)
                    mk = P.emit("scalar", fn, wait=cwp + (rr.pw() if tokg == 0 else []), sig="qA")
                    B.r_ps6.release(mk)
                    if tokg == 3:
                        rr.pd(mk)
                        scr = T["s_q"] if typ == 0 else T["s_v"]
                        dsem = "dsq" if typ == 0 else "dsv"
                        mk2 = P.emit("sync", lambda e, scr=scr, hh=hh, sls=sls, stg=stg: e.dma_start(
                            out=scr.ap()[hh], in_=stg[sls][:]),
                               wait=rr.cw(), sig=f"{dsem}{sls}")
                        rr.release(mk2)


def emit_attn(P, B, T, layer, seq):

    W = B.WORK
    scsb = [B.f32v(W, 2048), B.f32v(W + 8192, 2048)]
    psbs = [B.bfv(W + 16384 + 4096 * i, 2048) for i in range(4)]
    pt3 = B.bfv(W + 32768, 8192).rearrange("p (k q) -> p k q", q=512)
    qts = [B.bfv(W + 49152, 512), B.bfv(W + 50176, 512)]
    osts = [B.bfv(W + 51200, 512), B.bfv(W + 52224, 512)]
    Daps = [B.bfv(W + 53248 + 256 * i, 128) for i in range(4)]
    vstage = [B.bfv(W + 54528, 2048), B.bfv(W + 58624, 2048)]
    kfm = B.kfm3()
    vtm = B.vtm()

    rvl = B.r_vld
    for h in range(H):
        slv = rvl.psl()
        mk = P.emit("sync", lambda e, h=h, slv=slv, vstage=vstage: e.dma_start(
            out=vstage[slv][:], in_=T["s_v"].ap()[h]),
               wait=rvl.pw() + ([("dsv0", P.cnt("dsv0")), ("dsv1", P.cnt("dsv1"))] if h == 0 else []), sig=f"dv{slv}")
        rvl.pd(mk)
        cwv = rvl.cw()
        vap = vstage[rvl.csl()]
        for pk in range(4):
            tpw = B.r_tp.pw()
            slot = B.r_tp.psl()
            mk = None
            for j in range(4):
                kc = 4 * pk + j
                mk = P.emit("tensor", lambda e, vap=vap, kc=kc, j=j, slot=slot: e.matmul(
                    out=B.tp2[slot][:, 128 * j:128 * (j + 1)], lhsT=vap[:, 128 * kc:128 * (kc + 1)],
                    rhs=B.identbf[:], start=(j == 0), stop=(j == 3)),
                       wait=(cwv if pk == 0 and j == 0 else []) + (tpw if j == 0 else []),
                       sig=("qT" if j == 3 else None))
            B.r_tp.pd(mk)
            if pk == 3:
                rvl.release(mk)
            mke = P.emit("scalar", lambda e, pk=pk, h=h, slot=slot, vtm=vtm: e.activation(
                out=vtm[:, 4 * pk:4 * pk + 4, h, :],
                in_=B.tp2[slot].rearrange("p (j t) -> p j t", j=4),
                func=AF.Identity, bias=0.0, scale=1.0),
                   wait=B.r_tp.cw(), sig="qA")
            B.r_tp.release(mke)

    rqt = B.r_qt
    rsc = B.r_sc
    rpb = B.r_pb
    rD = B.r_D
    rot = B.r_ost
    B._prev_pv = None
    for g in range(4):
        for h in range(H):
            slq = rqt.psl()
            mk = P.emit("sync", lambda e, h=h, g=g, slq=slq, qts=qts: e.dma_start(
                out=qts[slq][:], in_=T["s_q"].ap()[h][:, 512 * g:512 * (g + 1)]),
                   wait=rqt.pw() + ([("dsq0", P.cnt("dsq0")), ("dsq1", P.cnt("dsq1"))] if (g == 0 and h == 0) else []),
                   sig=f"dq{slq}")
            rqt.pd(mk)
            cwq = rqt.cw()
            qap = qts[rqt.csl()]
            tlinfo = []
            prev_mmax = []
            for tl in range(4):
                sls = rsc.psl()
                scap = scsb[sls]
                rsc_pw = rsc.pw() + prev_mmax
                ev_mk = None
                for j in range(g + 1):
                    spw = B.r_sc4.pw()
                    slot = B.r_sc4.psl()
                    pap = B.acc6[slot]
                    mm = P.emit("tensor", lambda e, pap=pap, qap=qap, tl=tl, h=h, j=j, kfm=kfm: e.matmul(
                        out=pap, lhsT=qap[:, 128 * tl:128 * (tl + 1)],
                        rhs=kfm[:, h, 512 * j:512 * (j + 1)], start=True, stop=True),
                           wait=(cwq if (tl == 0 and j == 0) else []) + spw, sig="qT")
                    B.r_sc4.pd(mm)
                    if tl == 3 and j == g:
                        rqt.release(mm)
                    cwp = B.r_sc4.cw()
                    if j < g:
                        ev_mk = P.emit("vector", lambda e, pap=pap, scap=scap, j=j: e.tensor_scalar(
                            out=scap[:, 512 * j:512 * (j + 1)], in0=pap, scalar1=1.0,
                            scalar2=NEGB, op0=ALU.mult, op1=ALU.max,
                            accum_out=B.gmax()[:, j:j + 1]),
                               wait=cwp + (rsc_pw if j == 0 else []), sig="qV")
                    else:
                        mtt = P.emit("vector", lambda e, pap=pap, scap=scap, j=j, tl=tl: e.tensor_tensor(
                            out=scap[:, 512 * j:512 * (j + 1)], in0=pap, in1=B.mask(tl), op=ALU.add),
                               wait=cwp + (rsc_pw if j == 0 else []), sig="qV")
                        ev_mk = P.emit("vector", lambda e, scap=scap, j=j: e.reduce_max(
                            out=B.gmax()[:, j:j + 1], in_=scap[:, 512 * j:512 * (j + 1)], axis=AX.X),
                               wait=[mtt], sig="qV")
                    B.r_sc4.release(ev_mk)
                rsc.pd(ev_mk)
                cws = rsc.cw()
                mmax = P.emit("vector", lambda e, g=g, tl=tl: e.reduce_max(
                    out=B.tiny[:, tl:tl + 1], in_=B.gmax()[:, 0:g + 1], axis=AX.X, negate=True),
                       wait=[ev_mk], sig="qV")
                prev_mmax = [mmax]
                pbw = rpb.pw()
                psl = rpb.psl()
                mexp = P.emit("scalar", lambda e, scap=scap, g=g, tl=tl, psl=psl, psbs=psbs: e.activation(
                    out=psbs[psl][0:128, 0:512 * (g + 1)], in_=scap[:, 0:512 * (g + 1)], func=AF.Exp,
                    bias=B.tiny[:, tl:tl + 1], scale=1.0,
                    accum_out=B.tiny[:, 4 + tl:5 + tl]),
                       wait=[mmax] + pbw + cws, sig="qA")
                rpb.pd(mexp)
                rsc.release(mexp)
                tlinfo.append([mmax, psl, None, mexp])
            for tl in range(4):
                mexp = tlinfo[tl][3]
                mrec = P.emit("vector", lambda e, tl=tl: e.reciprocal(
                    out=B.tiny[:, 8 + tl:9 + tl], in_=B.tiny[:, 4 + tl:5 + tl]),
                       wait=[mexp], sig="qV")
                rdw = rD.pw()
                dsl = rD.psl()
                mD = P.emit("vector", lambda e, tl=tl, dsl=dsl, Daps=Daps: e.tensor_scalar(
                    out=Daps[dsl], in0=B.identbf[:], scalar1=B.tiny[:, 8 + tl:9 + tl],
                    scalar2=None, op0=ALU.mult),
                       wait=[mrec] + rdw, sig="qV")
                rD.pd(mD)
                tlinfo[tl][2] = (dsl, mD)
            nkc = 4 * (g + 1)
            prev_pv = getattr(B, "_prev_pv", None)
            ptw = [dict() for _ in range(nkc // 4)]
            for tl in range(4):
                _, psl, (dsl, mD), _ = tlinfo[tl]
                cwpb = rpb.cw()
                for pk in range(nkc // 4):
                    tpw = B.r_tp.pw()
                    slot = B.r_tp.psl()
                    mk = None
                    for j in range(4):
                        kc = 4 * pk + j
                        mk = P.emit("tensor", lambda e, slot=slot, kc=kc, j=j, psl=psl, dsl=dsl, psbs=psbs, Daps=Daps: e.matmul(
                            out=B.tp2[slot][:, 128 * j:128 * (j + 1)],
                            lhsT=psbs[psl][0:128, 128 * kc:128 * (kc + 1)],
                            rhs=Daps[dsl], start=(j == 0), stop=(j == 3)),
                               wait=((cwpb + [mD]) if (pk == 0 and j == 0) else []) + (tpw if j == 0 else []),
                               sig=("qT" if j == 3 else None))
                    B.r_tp.pd(mk)
                    if pk == nkc // 4 - 1:
                        rpb.release(mk)
                        rD.release(mk)
                    eng = "qA" if (pk + tl) % 2 == 0 else "qV"
                    war = []
                    if prev_pv is not None and eng not in ptw[pk]:
                        war = [prev_pv[min(4 * pk + 3, len(prev_pv) - 1)]]
                    if eng == "qA":
                        mke = P.emit("scalar", lambda e, pk=pk, tl=tl, slot=slot, pt3=pt3: e.activation(
                            out=pt3[:, 4 * pk:4 * pk + 4, 128 * tl:128 * (tl + 1)],
                            in_=B.tp2[slot].rearrange("p (j t) -> p j t", j=4),
                            func=AF.Identity, bias=0.0, scale=1.0),
                               wait=B.r_tp.cw() + war, sig="qA")
                    else:
                        mke = P.emit("vector", lambda e, pk=pk, tl=tl, slot=slot, pt3=pt3: e.tensor_copy(
                            out=pt3[:, 4 * pk:4 * pk + 4, 128 * tl:128 * (tl + 1)],
                            in_=B.tp2[slot].rearrange("p (j t) -> p j t", j=4)),
                               wait=B.r_tp.cw() + war, sig="qV")
                    ptw[pk][eng] = mke
                    B.r_tp.release(mke)
            obw = B.r_ob.pw()
            oslot = B.r_ob.psl()
            opap = [B.pb[:], B.pc[:]][oslot]
            pv_marks = []
            for kc in range(nkc):
                w = (list(ptw[kc // 4].values()) if kc % 4 == 0 else []) + (obw if kc == 0 else [])
                mpv = P.emit("tensor", lambda e, opap=opap, kc=kc, h=h, vtm=vtm, pt3=pt3, nkc=nkc: e.matmul(
                    out=opap, lhsT=vtm[:, kc, h, :], rhs=pt3[:, kc, :],
                    start=(kc == 0), stop=(kc == nkc - 1)),
                       wait=w, sig="qT")
                pv_marks.append(mpv)
            B._prev_pv = pv_marks
            B.r_ob.pd(mpv)
            cwo = B.r_ob.cw()
            oslr = rot.psl()
            mk = P.emit("scalar", lambda e, opap=opap, oslr=oslr, osts=osts: e.activation(
                out=osts[oslr][:], in_=opap, func=AF.Identity, bias=0.0, scale=1.0),
                   wait=cwo + rot.pw(), sig="qA")
            rot.pd(mk)
            B.r_ob.release(mk)
            dst_o = T["s_o"].ap()[g, :, :, h, :].rearrange("tq p q -> p tq q")
            mk2 = P.emit("sync", lambda e, dst_o=dst_o, oslr=oslr, osts=osts: e.dma_start(
                out=dst_o, in_=osts[oslr][:].rearrange("p (tq q) -> p tq q", q=128)),
                   wait=rot.cw(), sig=f"dso{oslr}")
            rot.release(mk2)


def emit_update(P, B, T, layer, seq, src_stream, dst_stream, wait_src=()):
    W = B.WORK
    ots = [B.bfv(W, 2048).rearrange("p (c t) -> p c t", c=16),
           B.bfv(W + 4096, 2048).rearrange("p (c t) -> p c t", c=16)]
    xts = [B.f32v(W + 8192, 2048), B.f32v(W + 16384, 2048)]
    t2s = [B.f32v(W + 24576, 2048), B.f32v(W + 32768, 2048)]
    wslab = B.bfv(B.KFM, 32768).rearrange("p (j c o) -> p j c o", j=4, o=512)
    rot = B.r_ot
    rxt = B.r_xt
    rt2 = B.r_t2

    wname = "s_woutT" if layer == 0 else "s_l1woT"
    for j in range(4):
        wmark = P.emit("sync", lambda e, j=j, wslab=wslab, wname=wname: e.dma_start(
            out=wslab[:, j, :, :].rearrange("p c o -> p (c o)"), in_=T[wname].ap()[j]),
               wait=(T["prep_marks"][wname] if j == 0 else []), sig="dw0")

    for tt in range(NT):
        slo = rot.psl()
        g, tq = tt // 4, tt % 4
        src_o = T["s_o"].ap()[g, tq]
        mko = P.emit("sync", lambda e, src_o=src_o, slo=slo, ots=ots: e.dma_start(
            out=ots[slo][:, :, :], in_=src_o),
               wait=rot.pw() + ([("dso0", P.cnt("dso0")), ("dso1", P.cnt("dso1"))] if tt == 0 else []), sig=f"do{slo}")
        rot.pd(mko)
        slx = rxt.psl()
        mkx = P.emit("sync", lambda e, tt=tt, slx=slx, xts=xts, src_stream=src_stream: e.dma_start(
            out=xts[slx][:], in_=src_stream[128 * tt:128 * (tt + 1), :]),
               wait=rxt.pw() + (list(wait_src) if tt == 0 else []), sig=f"dx{slx}")
        rxt.pd(mkx)
        cwo = rot.cw()
        oap = ots[rot.csl()]
        yslots = [[B.pa[:, 512 * og:512 * (og + 1)] for og in range(4)],
                  [B.pb[:], B.pc[:], B.pd[:], B.pe[:]]]
        yw = B.r_y.pw()
        ysl = B.r_y.psl()
        yaps = yslots[ysl]
        mmy = None
        for og in range(4):
            for cch in range(NCH):
                P.emit("tensor", lambda e, og=og, cch=cch, oap=oap, wslab=wslab, yaps=yaps: e.matmul(
                    out=yaps[og], lhsT=oap[:, cch, :],
                    rhs=wslab[:, og, cch, :], start=(cch == 0), stop=False),
                       wait=(cwo + yw + [wmark] if og == 0 and cch == 0 else []))
            mmy = P.emit("tensor", lambda e, og=og, layer=layer, yaps=yaps: e.matmul(
                out=yaps[og], lhsT=B.onesbf[0:1, :],
                rhs=B.brow[0:1, DM * layer + 512 * og:DM * layer + 512 * (og + 1)],
                start=False, stop=True),
                   sig=("qT" if og == 3 else None))
        B.r_y.pd(mmy)
        rot.release(mmy)
        ssl = tt % 2
        rw = emit_stats(P, B, yaps, ssl, wait0=[mmy])
        slt = rt2.psl()
        if ysl == 0:
            mka = P.emit("vector", lambda e, slt=slt, ssl=ssl, t2s=t2s: e.tensor_scalar(
                out=t2s[slt][:], in0=B.pa[:, 0:2048], scalar1=B.mv(ssl)[:, 0:1], scalar2=B.rr(ssl),
                op0=ALU.subtract, op1=ALU.mult),
                   wait=[rw] + rt2.pw(), sig="qV")
        else:
            for og in range(4):
                mka = P.emit("vector", lambda e, slt=slt, ssl=ssl, t2s=t2s, og=og, yaps=yaps: e.tensor_scalar(
                    out=t2s[slt][:, 512 * og:512 * (og + 1)], in0=yaps[og],
                    scalar1=B.mv(ssl)[:, 0:1], scalar2=B.rr(ssl),
                    op0=ALU.subtract, op1=ALU.mult),
                       wait=([rw] + rt2.pw() if og == 0 else []), sig="qV")
        rt2.pd(mka)
        B.r_y.release(mka)
        cwx = rxt.cw()
        xap = xts[rxt.csl()]
        mkadd = P.emit("vector", lambda e, slt=slt, xap=xap, t2s=t2s: e.tensor_tensor(
            out=t2s[slt][:], in0=t2s[slt][:], in1=xap, op=ALU.add),
               wait=cwx + [mka], sig="qV")
        rxt.release(mkadd)
        mks = P.emit("sync", lambda e, tt=tt, slt=slt, t2s=t2s, dst_stream=dst_stream: e.dma_start(
            out=dst_stream[128 * tt:128 * (tt + 1), :], in_=t2s[slt][:]),
               wait=[mkadd] + rt2.cw(), sig=f"dst{slt}")
        rt2.release(mks)


def emit_f1(P, B, T, layer, seq):
    W = B.WORK
    mats = ["s_gateT", "s_upT"] if layer == 0 else ["s_l1upT"]
    nm = len(mats)
    if layer == 0:
        wsl = [[B.bfv(B.KFM + (2 * r + m) * 16384, 8192).rearrange("p (c i) -> p c i", c=16)
                for m in range(nm)] for r in range(2)]
    else:
        wsl = [[B.bfv(B.KFM + r * 16384, 8192).rearrange("p (c i) -> p c i", c=16)] for r in range(2)]
    sgs = [B.bfv(W, 512), B.bfv(W + 1024, 512)]
    uss = [B.bfv(W + 2048, 512), B.bfv(W + 3072, 512)]
    tst = [B.bfv(W + 4096, 2048), B.bfv(W + 8192, 2048)]
    hfm = B.hfm3()
    rw = B.r_w2
    rsg = B.r_sg
    rus = B.r_us
    rts = B.r_ts
    for iw in range(16):
        slw = rw.psl()
        mk = None
        for m, mn in enumerate(mats):
            src = T[mn].ap()[iw].rearrange("p (c i) -> p c i", i=512)
            mk = P.emit("sync", lambda e, src=src, slw=slw, m=m, wsl=wsl: e.dma_start(
                out=wsl[slw][m][:, :, :], in_=src),
                   wait=(rw.pw() if m == 0 else [])
                        + (T["prep_marks"][mn] if iw == 0 else []),
                   sig=f"dw{slw}")
        rw.pd(mk)
        cww = rw.cw()
        first = True
        for icl in range(4):
            ich = 4 * iw + icl
            slt = rts.psl()
            mlast = None
            for tokg in range(4):
                groups = []
                for m in range(nm):
                    pw = B.r_ps6.pw()
                    slot = B.r_ps6.psl()
                    pap = B.acc6[slot]
                    mm = None
                    for cch in range(NCH):
                        mm = P.emit("tensor", lambda e, pap=pap, slw=slw, m=m, cch=cch, icl=icl, tokg=tokg, wsl=wsl, hfm=hfm: e.matmul(
                            out=pap, lhsT=wsl[slw][m][:, cch, 128 * icl:128 * (icl + 1)],
                            rhs=hfm[:, cch, 512 * tokg:512 * (tokg + 1)],
                            start=(cch == 0), stop=(cch == NCH - 1)),
                               wait=(cww if first else []) + (pw if cch == 0 else []),
                               sig=("qT" if cch == NCH - 1 else None))
                        first = False
                    B.r_ps6.pd(mm)
                    if icl == 3 and tokg == 3 and m == nm - 1:
                        rw.release(mm)
                    groups.append((B.r_ps6.cw(), pap))
                if layer == 0:
                    (cwg, gap), (cwu, uap) = groups
                    slg = rsg.psl()
                    mg = P.emit("scalar", lambda e, gap=gap, slg=slg, sgs=sgs: e.activation(
                        out=sgs[slg][:], in_=gap, func=AF.Silu, bias=0.0, scale=1.0),
                           wait=cwg + rsg.pw(), sig="qA")
                    rsg.pd(mg)
                    B.r_ps6.release(mg)
                    slu = rus.psl()
                    mu = P.emit("vector", lambda e, uap=uap, slu=slu, uss=uss: e.tensor_copy(
                        out=uss[slu][:], in_=uap),
                           wait=cwu + rus.pw(), sig="qV")
                    rus.pd(mu)
                    B.r_ps6.release(mu)
                    mlast = P.emit("vector", lambda e, slg=slg, slu=slu, slt=slt, tokg=tokg, sgs=sgs, uss=uss, tst=tst: e.tensor_tensor(
                        out=tst[slt][:, 512 * tokg:512 * (tokg + 1)], in0=sgs[slg][:],
                        in1=uss[slu][:], op=ALU.mult),
                           wait=rsg.cw() + rus.cw() + (rts.pw() if tokg == 0 else []), sig="qV")
                    rsg.release(mlast)
                    rus.release(mlast)
                else:
                    (cwu, uap) = groups[0]
                    mlast = P.emit("scalar", lambda e, uap=uap, slt=slt, tokg=tokg, tst=tst: e.activation(
                        out=tst[slt][:, 512 * tokg:512 * (tokg + 1)], in_=uap, func=AF.Relu,
                        bias=0.0, scale=1.0),
                           wait=cwu + (rts.pw() if tokg == 0 else []), sig="qA")
                    B.r_ps6.release(mlast)
            rts.pd(mlast)
            mk2 = P.emit("gpsimd", lambda e, ich=ich, slt=slt, tst=tst: e.dma_start(
                out=T["s_t"].ap()[ich], in_=tst[slt][:]),
                   wait=rts.cw(), sig=f"dstt{slt}")
            rts.release(mk2)


def emit_f2(P, B, T, layer, seq, src_stream, dst_stream, wait_src=()):
    W = B.WORK
    xts = [B.f32v(W, 2048), B.f32v(W + 8192, 2048)]
    t2s = [B.f32v(W + 16384, 2048)]
    ysb = [B.f32v(W + 24576, 512), B.f32v(W + 26624, 512)]
    wdn = "s_downT" if layer == 0 else "s_l1downT"
    wds = [B.bfv(B.KFM + r * 16384, 8192).rearrange("p (i o) -> p i o", i=64) for r in range(2)]
    ytm = B.f32v(B.KFM + 32768, 8192).rearrange("p (t o) -> p t o", t=4)
    treg = B.treg()
    rtr = B.r_tr
    rwd = B.r_wd
    rys = B.r_ys
    rytm = B.r_ytm
    rxt = B.r_xt
    rt2 = B.r_t2f
    for tr in range(4):
        src_t = T["s_t"].ap()[:, :, 512 * tr:512 * (tr + 1)].rearrange("i p t -> p i t")
        mtr = P.emit("sync", lambda e, src_t=src_t, treg=treg: e.dma_start(out=treg[:, :, :], in_=src_t),
               wait=rtr.pw() + ([("dstt0", P.cnt("dstt0")), ("dstt1", P.cnt("dstt1"))] if tr == 0 else []), sig="dt")
        rtr.pd(mtr)
        cwt = rtr.cw()
        ytw = rytm.pw()
        mke = None
        for och in range(16):
            slw = rwd.psl()
            src_w = T[wdn].ap()[och].rearrange("p (i o) -> p i o", o=128)
            mkw = P.emit("sync", lambda e, src_w=src_w, slw=slw, wds=wds: e.dma_start(
                out=wds[slw][:, :, :], in_=src_w),
                   wait=rwd.pw() + (T["prep_marks"][wdn] if (tr == 0 and och == 0) else []),
                   sig=f"dw{slw}")
            rwd.pd(mkw)
            cww = rwd.cw()
            obw = B.r_ob.pw()
            oslot = B.r_ob.psl()
            opap = [B.pb[:], B.pc[:]][oslot]
            mdown = None
            for ich in range(NICH):
                mdown = P.emit("tensor", lambda e, opap=opap, slw=slw, ich=ich, wds=wds, treg=treg: e.matmul(
                    out=opap, lhsT=wds[slw][:, ich, :], rhs=treg[:, ich, :],
                    start=(ich == 0), stop=(ich == NICH - 1)),
                       wait=(((cwt if och == 0 else []) + cww + obw) if ich == 0 else []),
                       sig=("qT" if ich == NICH - 1 else None))
            B.r_ob.pd(mdown)
            rwd.release(mdown)
            if och == 15:
                rtr.release(mdown)
            cwo = B.r_ob.cw()
            sly = rys.psl()
            mys = P.emit("scalar", lambda e, opap=opap, sly=sly, ysb=ysb: e.copy(
                out=ysb[sly][:], in_=opap),
                   wait=cwo + rys.pw(), sig="qA")
            rys.pd(mys)
            B.r_ob.release(mys)
            cwy = rys.cw()
            yap = ysb[rys.csl()]
            tpw = B.r_tp.pw()
            slot = B.r_tp.psl()
            mk = None
            for j in range(4):
                mk = P.emit("tensor", lambda e, slot=slot, yap=yap, j=j: e.matmul(
                    out=B.tp2[slot][:, 128 * j:128 * (j + 1)], lhsT=yap[:, 128 * j:128 * (j + 1)],
                    rhs=B.identf[:], start=(j == 0), stop=(j == 3)),
                       wait=((cwy + tpw) if j == 0 else []),
                       sig=("qT" if j == 3 else None))
            B.r_tp.pd(mk)
            rys.release(mk)
            mke = P.emit("vector", lambda e, och=och, slot=slot, ytm=ytm: e.tensor_copy(
                out=ytm[:, 0:4, 128 * och:128 * (och + 1)],
                in_=B.tp2[slot].rearrange("p (j t) -> p j t", j=4)),
                   wait=B.r_tp.cw() + (ytw if och == 0 else []), sig="qV")
            B.r_tp.release(mke)
        rytm.pd(mke)
        cwytm = rytm.cw()
        mlast_apply = None
        for ttl in range(4):
            tt = 4 * tr + ttl
            slx = rxt.psl()
            mkx = P.emit("sync", lambda e, tt=tt, slx=slx, xts=xts, src_stream=src_stream: e.dma_start(
                out=xts[slx][:], in_=src_stream[128 * tt:128 * (tt + 1), :]),
                   wait=rxt.pw() + (list(wait_src) if tt == 0 else []), sig=f"dx{slx}")
            rxt.pd(mkx)
            ssl = ttl % 2
            yrow = ytm[:, ttl, :]
            rw = emit_stats(P, B, [yrow[:, 512 * gg:512 * (gg + 1)] for gg in range(4)], ssl,
                            wait0=(cwytm if ttl == 0 else []))
            mka = P.emit("vector", lambda e, yrow=yrow, ssl=ssl, t2s=t2s: e.tensor_scalar(
                out=t2s[0][:], in0=yrow, scalar1=B.mv(ssl)[:, 0:1], scalar2=B.rr(ssl),
                op0=ALU.subtract, op1=ALU.mult),
                   wait=[rw] + rt2.pw(), sig="qV")
            rt2.pd(mka)
            mlast_apply = mka
            cwx = rxt.cw()
            xap = xts[rxt.csl()]
            mkadd = P.emit("vector", lambda e, xap=xap, t2s=t2s: e.tensor_tensor(
                out=t2s[0][:], in0=t2s[0][:], in1=xap, op=ALU.add),
                   wait=cwx + [mka], sig="qV")
            rxt.release(mkadd)
            mks = P.emit("sync", lambda e, tt=tt, t2s=t2s, dst_stream=dst_stream: e.dma_start(
                out=dst_stream[128 * tt:128 * (tt + 1), :], in_=t2s[0][:]),
                   wait=[mkadd] + rt2.cw(), sig="dstf")
            rt2.release(mks)
        rytm.release(mlast_apply)


def emit_prep(P, B, T, which):
    W = B.WORK
    pins = [B.f32v(W, 8192), B.f32v(W + 32768, 8192)]
    pini = [B.i32v(W, 8192), B.i32v(W + 32768, 8192)]
    pbfs = [B.bfv(B.KFM, 8192), B.bfv(B.KFM + 16384, 8192)]
    stgs = [B.bfv(B.KFM + 32768, 8192), B.bfv(B.KFM + 49152, 8192)]
    rpin = B.r_pin
    rpbf = B.r_pbf
    rstg = B.r_stg

    sc_specs = [("L0_ffn_gate_scale", 64, 0), ("L0_ffn_up_scale", 64, 64),
                ("L0_ffn_down_scale", 16, 128), ("L1_ffn_up_scale", 64, 144),
                ("L1_ffn_down_scale", 16, 208)]
    tmp = B.f32v(W, 5 * 128).rearrange("p (a b) -> p a b", b=128)
    for idx, (nm, ncol, base) in enumerate(sc_specs):
        src = T[nm].ap().rearrange("(c p) -> c p", p=128)
        P.emit("sync", lambda e, src=src, ncol=ncol, idx=idx, tmp=tmp: e.dma_start(
            out=tmp[0:ncol, idx, :], in_=src), sig="dsc")
    P.emit("sync", lambda e: e.dma_start(out=B.tiny[0:1, 6:7],
                                         in_=T["L1_ffn_output_scale"].ap().rearrange("(o a) -> o a", o=1)),
           sig="dsc")
    ld_all = [("dsc", P.cnt("dsc"))]
    mev = None
    for idx, (nm, ncol, base) in enumerate(sc_specs):
        mm = P.emit("tensor", lambda e, ncol=ncol, idx=idx, tmp=tmp: e.matmul(
            out=B.pd[:, 0:ncol], lhsT=tmp[0:ncol, idx, :],
            rhs=B.identf[0:ncol, 0:ncol], start=True, stop=True),
               wait=ld_all + ([mev] if mev else []), sig="qT")
        mev = P.emit("vector", lambda e, ncol=ncol, base=base: e.tensor_copy(
            out=B.scT[:, base:base + ncol], in_=B.pd[:, 0:ncol]),
               wait=[mm], sig="qV")
    mm = P.emit("tensor", lambda e: e.matmul(out=B.pd[:, 0:1], lhsT=B.onesf[0:1, :], rhs=B.tiny[0:1, 6:7],
                                             start=True, stop=True),
           wait=ld_all + [mev], sig="qT")
    mev = P.emit("vector", lambda e: e.tensor_copy(out=B.osb[:], in_=B.pd[:, 0:1]),
           wait=[mm], sig="qV")
    mos = P.emit("vector", lambda e: e.tensor_scalar(out=B.scT[:, 208:224], in0=B.scT[:, 208:224],
                                                     scalar1=B.osb[:], scalar2=None, op0=ALU.mult),
           wait=[mev], sig="qV")
    scale_ready = [mos]
    tp_gate = [mos]

    specs = {
        "s_qkvT": ("L0_attn_Wqkv", "cast", 6144, 2048, None),
        "s_woutT": ("L0_attn_Wout", "cast", 2048, 2048, None),
        "s_gateT": ("L0_ffn_gate_w", "dequant", 8192, 2048, 0),
        "s_upT": ("L0_ffn_up_w", "dequant", 8192, 2048, 64),
        "s_downT": ("L0_ffn_down_w", "dequant", 2048, 8192, 128),
        "s_l1upT": ("L1_ffn_up_weight", "sign", 8192, 2048, 144),
        "s_l1downT": ("L1_ffn_down_weight", "sign", 2048, 8192, 208),
        "s_l1qT": (None, "l1cast", None, None, None),
        "s_l1woT": (None, "l1cast", None, None, None),
    }

    def emit_l1cast(dn):
        dst = T[dn].ap()
        if dn == "s_l1qT":
            for m, mn in enumerate(["L1_attn_Wq", "L1_attn_Wk", "L1_attn_Wv"]):
                for h0 in range(0, H, 4):
                    slb = rpbf.psl()
                    src = T[mn].ap()[h0:h0 + 4].rearrange("h (cc p) d -> p h cc d", p=128)
                    mk = P.emit("gpsimd", lambda e, src=src, slb=slb: e.dma_start(
                        out=pbfs[slb][0:128, 0:8192].rearrange("p (h cc d) -> p h cc d", h=4, d=128),
                        in_=src),
                           wait=rpbf.pw(), sig=f"dp{slb}")
                    if m == 0:
                        mk = P.emit("vector", lambda e, slb=slb, pbfs=pbfs: e.tensor_scalar(
                            out=pbfs[slb][0:128, 0:8192], in0=pbfs[slb][0:128, 0:8192],
                            scalar1=QSCALE, scalar2=None, op0=ALU.mult),
                               wait=[mk], sig="qV")
                    rpbf.pd(mk)
                    cwb = rpbf.cw()
                    dap = dst[m * 16 + h0:m * 16 + h0 + 4].rearrange("h p r -> p h r")
                    mk2 = P.emit("gpsimd", lambda e, dap=dap, slb=slb: e.dma_start(
                        out=dap, in_=pbfs[slb][0:128, 0:8192].rearrange("p (h r) -> p h r", h=4)),
                           wait=cwb, sig=f"dpl{slb}")
                    rpbf.release(mk2)
        else:
            for c0 in range(0, 16, 4):
                slb = rpbf.psl()
                src = T["L1_attn_Wo"].ap()[128 * c0:128 * (c0 + 4), :].rearrange(
                    "(cc p) o -> p cc o", p=128)
                mk = P.emit("gpsimd", lambda e, src=src, slb=slb: e.dma_start(
                    out=pbfs[slb][0:128, 0:8192].rearrange("p (cc o) -> p cc o", o=2048), in_=src),
                       wait=rpbf.pw(), sig=f"dp{slb}")
                rpbf.pd(mk)
                cwb = rpbf.cw()
                for j in range(4):
                    dap = dst[j][:, 512 * c0:512 * (c0 + 4)].rearrange("p (cc o) -> p cc o", o=512)
                    sap = pbfs[slb][0:128, 0:8192].rearrange(
                        "p (cc j o) -> p cc j o", cc=4, o=512)[:, :, j, :]
                    mk2 = P.emit("gpsimd", lambda e, dap=dap, sap=sap: e.dma_start(out=dap, in_=sap),
                           wait=(cwb if j == 0 else []), sig=f"dpl{slb}")
                rpbf.release(mk2)
        T["prep_marks"][dn] = [("dpl0", P.cnt("dpl0")), ("dpl1", P.cnt("dpl1"))]

    for dn in which:
        srcn, mode, OUT, IN, scol = specs[dn]
        if mode == "l1cast":
            emit_l1cast(dn)
            continue
        src = T[srcn].ap()
        dst = T[dn].ap()
        for rt in range(OUT // 128):
            sli = rpin.psl()
            if mode == "cast":
                slb = rpbf.psl()
                mk = P.emit("gpsimd", lambda e, src=src, rt=rt, slb=slb, IN=IN, pbfs=pbfs: e.dma_start(
                    out=pbfs[slb][0:128, 0:IN], in_=src[128 * rt:128 * (rt + 1), :]),
                       wait=rpbf.pw() + (scale_ready if rt == 0 else []), sig=f"dp{slb}")
                if dn == "s_qkvT" and rt < 16:
                    mk = P.emit("vector", lambda e, slb=slb, IN=IN, pbfs=pbfs: e.tensor_scalar(
                        out=pbfs[slb][0:128, 0:IN], in0=pbfs[slb][0:128, 0:IN],
                        scalar1=QSCALE, scalar2=None, op0=ALU.mult),
                           wait=[mk], sig="qV")
                rpbf.pd(mk)
            elif mode == "dequant":
                mk = P.emit("gpsimd", lambda e, src=src, rt=rt, sli=sli, IN=IN, pini=pini: e.dma_start(
                    out=pini[sli][0:128, 0:IN], in_=src[128 * rt:128 * (rt + 1), :]),
                       wait=rpin.pw(), sig=f"dpi{sli}")
                rpin.pd(mk)
                cwi = rpin.cw()
                slb = rpbf.psl()
                scap = B.scT[:, scol + rt:scol + rt + 1]
                mk = P.emit("vector", lambda e, sli=sli, slb=slb, IN=IN, scap=scap, pini=pini, pbfs=pbfs: e.tensor_scalar(
                    out=pbfs[slb][0:128, 0:IN], in0=pini[sli][0:128, 0:IN],
                    scalar1=-128.0, scalar2=scap, op0=ALU.add, op1=ALU.mult),
                       wait=cwi + rpbf.pw() + (scale_ready if rt == 0 else []), sig="qV")
                rpbf.pd(mk)
                rpin.release(mk)
            else:
                mdma = P.emit("gpsimd", lambda e, src=src, rt=rt, IN=IN, pins=pins: e.dma_start(
                    out=pins[0][0:128, 0:IN], in_=src[128 * rt:128 * (rt + 1), :]),
                       wait=([("qA", P.cnt("qA"))] if rt > 0 else []) + (scale_ready if rt == 0 else []),
                       sig="dps")
                slb = rpbf.psl()
                scap = B.scT[:, scol + rt:scol + rt + 1]
                sgntmp = B.bfv(B.WORK + 32768, 8192)
                msgn = P.emit("scalar", lambda e, IN=IN, pins=pins, sgntmp=sgntmp: e.activation(
                    out=sgntmp[:, 0:IN], in_=pins[0][0:128, 0:IN], func=AF.Sign,
                    bias=0.0, scale=1.0),
                       wait=[mdma], sig="qA")
                mk = P.emit("vector", lambda e, slb=slb, IN=IN, scap=scap, sgntmp=sgntmp, pbfs=pbfs: e.tensor_scalar(
                    out=pbfs[slb][0:128, 0:IN], in0=sgntmp[:, 0:IN],
                    scalar1=scap, scalar2=None, op0=ALU.mult),
                       wait=[msgn] + rpbf.pw() + (scale_ready if rt == 0 else []), sig="qV")
                rpbf.pd(mk)
            cwb = rpbf.cw()
            bap = pbfs[rpbf.csl()]
            wsz = 1 if IN == 8192 else 4
            if rt % wsz == 0:
                sls = rstg.psl()
                first_v = True
                first_a = True
            npk = IN // 512
            mke = None
            for pk in range(npk):
                tpw = B.r_tp.pw()
                slot = B.r_tp.psl()
                mk = None
                for j in range(4):
                    c = 4 * pk + j
                    mk = P.emit("tensor", lambda e, slot=slot, bap=bap, c=c, j=j: e.matmul(
                        out=B.tp2[slot][:, 128 * j:128 * (j + 1)], lhsT=bap[0:128, 128 * c:128 * (c + 1)],
                        rhs=B.identbf[:], start=(j == 0), stop=(j == 3)),
                           wait=(cwb if pk == 0 and j == 0 else []) + (tpw if j == 0 else [])
                                + (tp_gate if (tp_gate and pk == 0 and j == 0) else []),
                           sig=("qT" if j == 3 else None))
                tp_gate = None
                B.r_tp.pd(mk)
                if pk == npk - 1:
                    rpbf.release(mk)
                if wsz == 1:
                    sdst = stgs[sls][:, 512 * pk:512 * (pk + 1)]
                    sr = B.tp2[slot]
                else:
                    sdst = stgs[sls].rearrange("p (c o) -> p c o", o=512)[:, 4 * pk:4 * pk + 4,
                                                                         128 * (rt % 4):128 * (rt % 4) + 128]
                    sr = B.tp2[slot].rearrange("p (j t) -> p j t", j=4)
                if pk % 2 == 0:
                    mke = P.emit("vector", lambda e, sdst=sdst, sr=sr: e.tensor_copy(out=sdst, in_=sr),
                           wait=B.r_tp.cw() + (rstg.pw() if first_v else []), sig="qV")
                    first_v = False
                else:
                    mke = P.emit("scalar", lambda e, sdst=sdst, sr=sr: e.activation(
                        out=sdst, in_=sr, func=AF.Identity, bias=0.0, scale=1.0),
                           wait=B.r_tp.cw() + (rstg.pw() if first_a else []), sig="qA")
                    first_a = False
                B.r_tp.release(mke)
            if rt % wsz == wsz - 1:
                rstg.pd(mke)
                rstg.cw()
                both = [("qV", P.cnt("qV")), ("qA", P.cnt("qA"))]
                dst_ap = dst[rt] if wsz == 1 else dst[rt // 4]
                mk2 = P.emit("gpsimd", lambda e, dst_ap=dst_ap, sls=sls: e.dma_start(
                    out=dst_ap, in_=stgs[sls][:, 0:8192]),
                       wait=both, sig=f"dpo{sls}")
                rstg.release(mk2)
        T["prep_marks"][dn] = [("dpo0", P.cnt("dpo0")), ("dpo1", P.cnt("dpo1"))]


def emit_bias_prep(P, B, T, layer):
    W = B.WORK
    tmp = B.f32v(W, 3 * 128).rearrange("p (a b) -> p a b", b=128)
    dn = f"db{layer}"
    if layer == 0:
        P.emit("sync", lambda e, tmp=tmp: e.dma_start(
            out=tmp[0:48, 0, :], in_=T["L0_attn_bqkv"].ap().rearrange("(c p) -> c p", p=128)),
               sig=dn)
        P.emit("gpsimd", lambda e: e.dma_start(out=B.brow[0:1, 0:DM],
                                               in_=T["L0_attn_bout"].ap().rearrange("(o a) -> o a", o=1)),
               sig=dn + "g")
        ld_all = [(dn, P.cnt(dn)), (dn + "g", P.cnt(dn + "g"))]
        mm = P.emit("tensor", lambda e, tmp=tmp: e.matmul(out=B.pd[:, 0:48], lhsT=tmp[0:48, 0, :],
                                                          rhs=B.identf[0:48, 0:48], start=True, stop=True),
               wait=ld_all, sig="qT")
        mbq = P.emit("vector", lambda e: e.tensor_copy(out=B.biasq[:, 0:48], in_=B.pd[:, 0:48]),
               wait=[mm], sig="qV")
        P.emit("vector", lambda e: e.tensor_scalar(out=B.biasq[:, 0:16], in0=B.biasq[:, 0:16],
                                                   scalar1=QSCALE, scalar2=None, op0=ALU.mult),
               wait=[mbq], sig="qV")
    else:
        for i, nm in enumerate(["L1_attn_bq", "L1_attn_bk", "L1_attn_bv"]):
            P.emit("sync", lambda e, nm=nm, i=i, tmp=tmp: e.dma_start(out=tmp[0:16, i, :], in_=T[nm].ap()),
                   sig=dn)
        P.emit("gpsimd", lambda e: e.dma_start(out=B.brow[0:1, DM:2 * DM],
                                               in_=T["L1_attn_bo"].ap().rearrange("(o a) -> o a", o=1)),
               sig=dn + "g")
        ld_all = [(dn, P.cnt(dn)), (dn + "g", P.cnt(dn + "g"))]
        mev = None
        for i in range(3):
            mm = P.emit("tensor", lambda e, i=i, tmp=tmp: e.matmul(out=B.pd[:, 0:16], lhsT=tmp[0:16, i, :],
                                                                   rhs=B.identf[0:16, 0:16], start=True, stop=True),
                   wait=ld_all + ([mev] if mev else []), sig="qT")
            mev = P.emit("vector", lambda e, i=i: e.tensor_copy(
                out=B.biasq[:, 48 + 16 * i:48 + 16 * (i + 1)], in_=B.pd[:, 0:16]),
                   wait=[mm], sig="qV")
            if i == 0:
                mev = P.emit("vector", lambda e: e.tensor_scalar(
                    out=B.biasq[:, 48:64], in0=B.biasq[:, 48:64],
                    scalar1=QSCALE, scalar2=None, op0=ALU.mult),
                       wait=[mev], sig="qV")


def emit_const_load(P, B, T):
    for nm, dst in [("c_identbf", B.identbf), ("c_identf", B.identf), ("c_onesbf", B.onesbf),
                    ("c_onesf", B.onesf), ("c_masks", B.masks)]:
        P.emit("sync", lambda e, nm=nm, dst=dst: e.dma_start(out=dst[:], in_=T[nm].ap()), sig="dc")
    for e in ENGINES:
        P.emit(e, None, wait=[("dc", P.cnt("dc"))])


INPUT_SPECS = {
    "L0_attn_Wqkv": ([3 * DM, DM], F32), "L0_attn_bqkv": ([3 * DM], F32),
    "L0_attn_Wout": ([DM, DM], F32), "L0_attn_bout": ([DM], F32),
    "L0_ffn_gate_w": ([DFF, DM], I32), "L0_ffn_gate_scale": ([DFF], F32),
    "L0_ffn_up_w": ([DFF, DM], I32), "L0_ffn_up_scale": ([DFF], F32),
    "L0_ffn_down_w": ([DM, DFF], I32), "L0_ffn_down_scale": ([DM], F32),
    "L1_attn_Wq": ([H, DM, HD], F32), "L1_attn_bq": ([H, HD], F32),
    "L1_attn_Wk": ([H, DM, HD], F32), "L1_attn_bk": ([H, HD], F32),
    "L1_attn_Wv": ([H, DM, HD], F32), "L1_attn_bv": ([H, HD], F32),
    "L1_attn_Wo": ([DM, DM], F32), "L1_attn_bo": ([DM], F32),
    "L1_ffn_up_weight": ([DFF, DM], F32), "L1_ffn_down_weight": ([DM, DFF], F32),
    "L1_ffn_up_scale": ([DFF], F32), "L1_ffn_down_scale": ([DM], F32),
    "L1_ffn_output_scale": ([1], F32),
}
CONST_SPECS = {
    "c_identbf": ([128, 128], BF16), "c_identf": ([128, 128], F32),
    "c_onesbf": ([1, 128], BF16), "c_onesf": ([1, 128], F32),
    "c_masks": ([128, 4 * 512], BF16),
}
SCRATCH_SPECS = {
    "s_qkvT": ([12, 128, 16 * 512], BF16),
    "s_woutT": ([4, 128, 16 * 512], BF16),
    "s_gateT": ([16, 128, 16 * 512], BF16),
    "s_upT": ([16, 128, 16 * 512], BF16),
    "s_downT": ([16, 128, 64 * 128], BF16),
    "s_l1qT": ([48, 128, 16 * 128], BF16),
    "s_l1woT": ([4, 128, 16 * 512], BF16),
    "s_l1upT": ([16, 128, 16 * 512], BF16),
    "s_l1downT": ([16, 128, 64 * 128], BF16),
}
LOCAL_SCRATCH = {
    "s_q": ([H, 128, S], BF16), "s_v": ([H, 128, S], BF16),
    "s_o": ([4, 4, 128, H, 128], BF16),
    "s_t": ([NICH, 128, S], BF16),
}


def build_program(cfg):
    nc = bass.Bass("TRN2", target_bir_lowering=False)
    ctx = ExitStack()
    T = {"prep_marks": {}}
    for n in cfg.get("inputs", []):
        sh, dt = INPUT_SPECS[n]
        T[n] = nc.dram_tensor(n, sh, dt, kind="ExternalInput")
    for n in CONST_SPECS:
        sh, dt = CONST_SPECS[n]
        T[n] = nc.dram_tensor(n, sh, dt, kind="ExternalInput")
    for n in cfg.get("scratch_in", []):
        sh, dt = SCRATCH_SPECS[n]
        T[n] = nc.dram_tensor(n, sh, dt, kind="ExternalInput")
        T["prep_marks"][n] = []
    for n in cfg.get("scratch_out", []):
        sh, dt = SCRATCH_SPECS[n]
        T[n] = nc.dram_tensor(n, sh, dt, kind="ExternalOutput")
    for n in cfg.get("scratch_internal", []):
        sh, dt = SCRATCH_SPECS[n]
        T[n] = nc.dram_tensor(n, sh, dt, kind="Internal")
        T["prep_marks"][n] = []
    for n, (sh, dt) in LOCAL_SCRATCH.items():
        T[n] = nc.dram_tensor(n, sh, dt, kind="Internal")
    if cfg.get("x_in"):
        T["x"] = nc.dram_tensor("x", [NSEQ, S, DM], F32, kind="ExternalInput")
    if cfg.get("stream_in"):
        T["stream_i"] = nc.dram_tensor("stream_i", [NSEQ, S, DM], F32, kind="ExternalInput")
    if cfg.get("stream_out"):
        T["stream_o"] = nc.dram_tensor("stream_o", [NSEQ, S, DM], F32, kind="ExternalOutput")
    if cfg.get("out"):
        T["out"] = nc.dram_tensor("out", [NSEQ, S, DM], F32, kind="ExternalOutput")
    if cfg.get("stream_internal"):
        T["stream_n"] = nc.dram_tensor("stream_n", [NSEQ, S, DM], F32, kind="Internal")

    plan = Plan(nc, ctx)
    B = Bufs(nc, ctx, plan)
    for nm, n in [("ps6", 6), ("sc4", 4), ("ob", 2), ("tp", 2), ("ry", 2), ("p1x", 2),
                  ("p1h", 2), ("w2", 2), ("qst", 2), ("vst", 2), ("vld", 2), ("qt", 2),
                  ("sc", 2), ("pbb", 4), ("rd", 4), ("ptb", 1), ("ostr", 2), ("ot", 2), ("xt", 2),
                  ("t2", 2), ("sg", 2), ("us", 2), ("ts", 2), ("tr", 1), ("wd", 2),
                  ("ys", 2), ("ytm", 1), ("t2f", 1), ("pin", 2), ("pbf", 2), ("stg", 2)]:
        setattr(B, "r_" + nm, Ring(plan, nm, n))
    B.r_y = B.r_ry
    B.r_pb = B.r_pbb
    B.r_D = B.r_rd
    B.r_ptb = B.r_ptb
    B.r_ost = B.r_ostr

    emit_const_load(plan, B, T)
    plan.barrier()

    stream_written = {}

    def src_dst(seq, blk_global):
        if cfg.get("x_in") and blk_global == cfg["first_blk"]:
            srcap = T["x"].ap()[seq]
        elif cfg.get("stream_in") and blk_global == cfg["first_blk"]:
            srcap = T["stream_i"].ap()[seq]
        else:
            srcap = T["stream_n"].ap()[seq] if cfg.get("stream_internal") else T["stream_o"].ap()[seq]
        if blk_global == cfg["last_blk"]:
            dstap = T["out"].ap()[seq] if cfg.get("out") else T["stream_o"].ap()[seq]
        else:
            dstap = T["stream_n"].ap()[seq] if cfg.get("stream_internal") else T["stream_o"].ap()[seq]
        return srcap, dstap

    for unit in cfg["units"]:
        if unit[0] == "prep":
            emit_prep(plan, B, T, unit[1])
            plan.barrier()
            continue
        _, blk = unit
        layer, kind = blk // 2, ("attn" if blk % 2 == 0 else "ffn")
        if kind == "attn":
            emit_bias_prep(plan, B, T, layer)
            plan.barrier()
        phs = cfg.get("phases")
        for seq in range(NSEQ):
            srcap, dstap = src_dst(seq, blk)
            wait_src = stream_written.get(seq, [])
            if phs is None or "p1" in phs:
                emit_p1(plan, B, T, srcap, wait_first=wait_src)
                plan.barrier()
            if kind == "attn":
                if phs is None or "qkv" in phs:
                    emit_qkv(plan, B, T, layer, seq)
                    plan.barrier()
                if phs is None or "attn" in phs:
                    emit_attn(plan, B, T, layer, seq)
                    plan.barrier()
                if phs is None or "update" in phs:
                    emit_update(plan, B, T, layer, seq, srcap, dstap, wait_src=wait_src)
                    plan.barrier()
            else:
                if phs is None or "f1" in phs:
                    emit_f1(plan, B, T, layer, seq)
                    plan.barrier()
                if phs is None or "f2" in phs:
                    emit_f2(plan, B, T, layer, seq, srcap, dstap, wait_src=wait_src)
                    plan.barrier()
            stream_written[seq] = [(n_, plan.cnt(n_)) for n_ in ("dst0", "dst1", "dstf")]

    for semn in ("dst0", "dst1", "dstf", "dpo0", "dpo1", "dpl0", "dpl1"):
        plan.emit("sync", None, wait=[(semn, plan.cnt(semn))])

    with ctx:
        with nc.Block() as block:
            def runner(ops):
                def f(e):
                    for op in ops:
                        op(e)
                return f
            block.sync(runner(plan.ops["sync"]))
            block.tensor(runner(plan.ops["tensor"]))
            block.vector(runner(plan.ops["vector"]))
            block.scalar(runner(plan.ops["scalar"]))
            block.gpsimd(runner(plan.ops["gpsimd"]))
    return nc, T


def host_consts():
    identbf = np.eye(128, dtype=BF)
    identf = np.eye(128, dtype=np.float32)
    onesbf = np.ones((1, 128), dtype=BF)
    onesf = np.ones((1, 128), dtype=np.float32)
    masks = np.zeros((128, 4 * 512), dtype=np.float32)
    q = np.arange(128)[:, None]
    for tl in range(4):
        kl = np.arange(512)[None, :]
        masks[:, 512 * tl:512 * (tl + 1)] = np.where(kl <= 128 * tl + q, 0.0, NEGB)
    return {"c_identbf": identbf, "c_identf": identf, "c_onesbf": onesbf,
            "c_onesf": onesf, "c_masks": masks.astype(BF)}


_PROG_CACHE = {}


def _get_prog(key, cfg):
    if key not in _PROG_CACHE:
        _PROG_CACHE[key] = build_program(cfg)
    return _PROG_CACHE[key]


ALL_SCRATCH = list(SCRATCH_SPECS.keys())
PREP_ALL = ("prep", ALL_SCRATCH)

LAUNCH_CFGS = {
    "prep": dict(
        inputs=["L0_attn_Wqkv", "L0_attn_Wout", "L0_ffn_gate_w", "L0_ffn_gate_scale",
                "L0_ffn_up_w", "L0_ffn_up_scale", "L0_ffn_down_w", "L0_ffn_down_scale",
                "L1_attn_Wq", "L1_attn_Wk", "L1_attn_Wv", "L1_attn_Wo",
                "L1_ffn_up_weight", "L1_ffn_down_weight", "L1_ffn_up_scale",
                "L1_ffn_down_scale", "L1_ffn_output_scale"],
        scratch_out=ALL_SCRATCH, units=[PREP_ALL], first_blk=-1, last_blk=-1),
    "b0": dict(inputs=["L0_attn_bqkv", "L0_attn_bout"], scratch_in=["s_qkvT", "s_woutT"],
               x_in=True, stream_out=True, units=[("blk", 0)], first_blk=0, last_blk=0),
    "b1": dict(inputs=[], scratch_in=["s_gateT", "s_upT", "s_downT"],
               stream_in=True, stream_out=True, units=[("blk", 1)], first_blk=1, last_blk=1),
    "b2": dict(inputs=["L1_attn_bq", "L1_attn_bk", "L1_attn_bv", "L1_attn_bo"],
               scratch_in=["s_l1qT", "s_l1woT"],
               stream_in=True, stream_out=True, units=[("blk", 2)], first_blk=2, last_blk=2),
    "b3": dict(inputs=[], scratch_in=["s_l1upT", "s_l1downT"],
               stream_in=True, out=True, units=[("blk", 3)], first_blk=3, last_blk=3),
}

FUSED_CFG = dict(
    inputs=[n for n in INPUT_SPECS],
    scratch_internal=ALL_SCRATCH,
    x_in=True, out=True, stream_internal=True,
    units=[PREP_ALL, ("blk", 0), ("blk", 1), ("blk", 2), ("blk", 3)],
    first_blk=0, last_blk=3)

MODE = "fused"


def kernel(**inputs):
    inputs = {k: np.asarray(v) for k, v in inputs.items()}
    x_full = inputs["L0_attn_x"]
    consts = host_consts()

    def run(name, extra_by_core):
        cfg = FUSED_CFG if name == "fused" else LAUNCH_CFGS[name]
        nc, T = _get_prog(name, cfg)
        in_maps = []
        for c in range(NCORE):
            m = dict(consts)
            for n in cfg.get("inputs", []):
                m[n] = inputs[n]
            for n in cfg.get("scratch_in", []):
                m[n] = extra_by_core[c][n]
            if cfg.get("x_in"):
                m["x"] = extra_by_core[c]["x"]
            if cfg.get("stream_in"):
                m["stream_i"] = extra_by_core[c]["stream_i"]
            in_maps.append(m)
        res = run_bass_kernel_spmd(nc, in_maps, core_ids=list(range(NCORE)))
        return res.results

    xs = [np.ascontiguousarray(x_full[2 * c:2 * c + 2]) for c in range(NCORE)]
    if MODE == "fused":
        r = run("fused", [dict(x=xs[c]) for c in range(NCORE)])
        out = np.concatenate([np.asarray(r[c]["out"]) for c in range(NCORE)], axis=0)
    else:
        prep_out = run("prep", [dict() for _ in range(NCORE)])
        extra = [dict(prep_out[c]) for c in range(NCORE)]
        for c in range(NCORE):
            extra[c]["x"] = xs[c]
        r = run("b0", extra)
        for c in range(NCORE):
            extra[c]["stream_i"] = np.asarray(r[c]["stream_o"])
        r = run("b1", extra)
        for c in range(NCORE):
            extra[c]["stream_i"] = np.asarray(r[c]["stream_o"])
        r = run("b2", extra)
        for c in range(NCORE):
            extra[c]["stream_i"] = np.asarray(r[c]["stream_o"])
        r = run("b3", extra)
        out = np.concatenate([np.asarray(r[c]["out"]) for c in range(NCORE)], axis=0)
    return out.astype(np.float32)


if __name__ == "__main__":
    import time
    for name in ["prep", "b0", "b1", "b2", "b3"]:
        t0 = time.time()
        nc, T = build_program(LAUNCH_CFGS[name])
        n = sum(1 for _ in nc.all_instructions())
        print(f"{name}: {n} instructions, build {time.time()-t0:.1f}s")


# version 10
# speedup vs baseline: 1.0279x; 1.0279x over previous
import numpy as np
import ml_dtypes
from contextlib import ExitStack

import concourse.bass as bass
import concourse.mybir as mybir
from concourse.bass_utils import run_bass_kernel_spmd

F32 = mybir.dt.float32
BF16 = mybir.dt.bfloat16
I32 = mybir.dt.int32
AF = mybir.ActivationFunctionType
ALU = mybir.AluOpType
AX = mybir.AxisListType

S = 2048
DM = 2048
H = 16
HD = 128
NT = 16
NCH = 16
DFF = 8192
NICH = 64
NSEQ = 2
NCORE = 8
QSCALE = float(HD) ** -0.5
EPS = 1e-5
NEGB = -3.0e38

ENGINES = ("sync", "tensor", "vector", "scalar", "gpsimd")
QSEM = {"tensor": "qT", "vector": "qV", "scalar": "qA", "gpsimd": "qG"}

BF = ml_dtypes.bfloat16


class Plan:
    def __init__(self, nc, ctx):
        self.nc = nc
        self.ctx = ctx
        self.ops = {e: [] for e in ENGINES}
        self._sems = {}
        self.nphase = 0

    def sem(self, name):
        if name not in self._sems:
            h = self.ctx.enter_context(self.nc.semaphore(name))
            self._sems[name] = [h, 0]
        return name

    def cnt(self, name):
        self.sem(name)
        return self._sems[name][1]

    def emit(self, eng, fn=None, wait=(), sig=None, dstep=16):
        wl = []
        for n, v in wait:
            self.sem(n)
            if v > 0:
                wl.append((self._sems[n][0], v))
        mark = None
        il = None
        if sig is not None:
            self.sem(sig)
            d = dstep if sig.startswith("d") else 1
            self._sems[sig][1] += d
            il = (self._sems[sig][0], d)
            mark = (sig, self._sems[sig][1])

        def op(e, wl=wl, il=il, fn=fn):
            for h, v in wl:
                e.wait_ge(h, v)
            ins = e.nop() if fn is None else fn(e)
            if il is not None:
                ins.then_inc(il[0], il[1])

        self.ops[eng].append(op)
        return mark

    def barrier(self):
        self.nphase += 1
        sems = ("qT", "qV", "qA", "qG",
                "dst0", "dst1", "dstf", "dsq0", "dsq1", "dsv0", "dsv1",
                "dso0", "dso1", "dstt0", "dstt1", "dpo0", "dpo1", "dpl0", "dpl1")
        w = [(q, self.cnt(q)) for q in sems if self.cnt(q) > 0]
        for e in ENGINES:
            self.emit(e, None, wait=w)


class Ring:

    def __init__(self, plan, name, n):
        self.p = plan
        self.n = n
        self.name = name
        self.marks = []
        self.rel = []
        self.ip = 0
        self.ic = 0

    def pw(self):
        i = self.ip
        if i < self.n:
            return []
        return [self.rel[i - self.n]]

    def pd(self, mark):
        assert mark is not None, f"ring {self.name}: producer op must signal"
        self.marks.append(mark)
        self.ip += 1

    def cw(self):
        i = self.ic
        self.ic += 1
        return [self.marks[i]]

    def release(self, mark):
        assert mark is not None, f"ring {self.name}: consumer op must signal"
        self.rel.append(mark)

    def psl(self):
        return self.ip % self.n

    def csl(self):
        return (self.ic - 1) % self.n


class Bufs:
    def __init__(self, nc, ctx, plan):
        self.nc = nc
        AR_F32 = 48 * 1024
        self.ar_f = ctx.enter_context(nc.sbuf_tensor("arena", [128, AR_F32], F32))
        self.ar_b = self.ar_f.bitcast(BF16)
        self.ar_i = self.ar_f.bitcast(I32)

        T = lambda n, s, d: ctx.enter_context(nc.sbuf_tensor(n, s, d))
        self.identbf = T("identbf", [128, 128], BF16)
        self.identf = T("identf", [128, 128], F32)
        self.onesbf = T("onesbf", [1, 128], BF16)
        self.onesf = T("onesf", [1, 128], F32)
        self.masks = T("masks", [128, 4 * 512], BF16)
        self.biasq = T("biasq", [128, 96], F32)
        self.brow = T("brow", [1, 2 * DM], BF16)
        self.stats = T("stats", [128, 64], F32)
        self.scT = T("scT", [128, 224], F32)
        self.osb = T("osb", [128, 1], F32)
        self.tiny = T("tiny", [128, 16], F32)

        self.pa = ctx.enter_context(nc.psum_tensor("pa", [128, 2048], F32))
        self.pb = ctx.enter_context(nc.psum_tensor("pb", [128, 512], F32))
        self.pc = ctx.enter_context(nc.psum_tensor("pc", [128, 512], F32))
        self.pd = ctx.enter_context(nc.psum_tensor("pd", [128, 512], F32))
        self.pe = ctx.enter_context(nc.psum_tensor("pe", [128, 512], F32))
        self.acc6 = [self.pa[:, 512 * j:512 * (j + 1)] for j in range(4)] + [self.pb[:], self.pc[:]]
        self.tp2 = [self.pd[:], self.pe[:]]

    def bfv(self, byte_off, nelem):
        assert byte_off % 2 == 0
        o = byte_off // 2
        return self.ar_b[:, o:o + nelem]

    def f32v(self, byte_off, nelem):
        assert byte_off % 4 == 0
        o = byte_off // 4
        return self.ar_f[:, o:o + nelem]

    def i32v(self, byte_off, nelem):
        o = byte_off // 4
        return self.ar_i[:, o:o + nelem]

    HFM = 0
    KFM = 65536
    WORK = 131072

    def hfm3(self):
        return self.bfv(self.HFM, 32768).rearrange("p (c t) -> p c t", t=S)

    def kfm3(self):
        return self.bfv(self.KFM, 32768).rearrange("p (c t) -> p c t", t=S)

    def vtm(self):
        return self.bfv(self.HFM, 32768).rearrange("p (k h d) -> p k h d", h=H, d=HD)

    def treg(self):
        return self.bfv(self.HFM, 32768).rearrange("p (i t) -> p i t", t=512)

    def bnb(self, sl):
        return self.stats[:, 24 * sl:24 * sl + 24]

    def mv(self, sl):
        return self.stats[:, 48 + 2 * sl:48 + 2 * sl + 2]

    def v1(self, sl):
        return self.stats[:, 52 + sl:53 + sl]

    def std(self, sl):
        return self.stats[:, 54 + sl:55 + sl]

    def rr(self, sl):
        return self.stats[:, 56 + sl:57 + sl]

    def gmax(self):
        return self.stats[:, 60:64]

    def mask(self, tl):
        return self.masks[:, 512 * tl:512 * (tl + 1)]


def emit_stats(P, B, srcs4, sl, wait0=()):
    m = None
    for g, s in enumerate(srcs4):
        m = P.emit("vector", lambda e, s=s, g=g, sl=sl: e.bn_stats(
            out=B.bnb(sl)[:, 6 * g:6 * g + 6], in_=s),
            wait=(list(wait0) if g == 0 else []), sig=("qV" if g == 3 else None))
    m = P.emit("vector", lambda e, sl=sl: e.bn_aggr(out=B.mv(sl), in_=B.bnb(sl)),
               wait=[m], sig="qV")
    m = P.emit("vector", lambda e, sl=sl: e.tensor_scalar(
        out=B.v1(sl), in0=B.mv(sl)[:, 1:2], scalar1=EPS, scalar2=None, op0=ALU.add),
               wait=[m], sig="qV")
    m = P.emit("scalar", lambda e, sl=sl: e.activation(
        out=B.std(sl), in_=B.v1(sl), func=AF.Sqrt, bias=0.0, scale=1.0),
               wait=[m], sig="qA")
    m = P.emit("vector", lambda e, sl=sl: e.reciprocal(out=B.rr(sl), in_=B.std(sl)),
               wait=[m], sig="qV")
    return m


def emit_p1(P, B, T, src_seq_ap, wait_first=()):
    W = B.WORK
    xts = [B.f32v(W, 2048), B.f32v(W + 8192, 2048)]
    hts = [B.bfv(W + 16384, 2048), B.bfv(W + 20480, 2048)]
    rx = B.r_p1x
    rh = B.r_p1h
    hfm = B.hfm3()
    for tt in range(NT):
        sl = rx.psl()
        mk = P.emit("sync", lambda e, t=tt, sl=sl, xts=xts, src_seq_ap=src_seq_ap: e.dma_start(
            out=xts[sl][:], in_=src_seq_ap[128 * t:128 * (t + 1), :]),
               wait=rx.pw() + (list(wait_first) if tt == 0 else []), sig=f"dx{sl}")
        rx.pd(mk)
        cwx = rx.cw()
        ssl = tt % 2
        xap = xts[rx.csl()]
        rw = emit_stats(P, B, [xap[:, 512 * g:512 * (g + 1)] for g in range(4)], ssl, wait0=cwx)
        hsl = rh.psl()
        mn = P.emit("vector", lambda e, ssl=ssl: e.tensor_scalar(
            out=B.stats[:, 58 + ssl:59 + ssl], in0=B.mv(ssl)[:, 0:1],
            scalar1=B.rr(ssl), scalar2=-1.0, op0=ALU.mult, op1=ALU.mult),
               wait=[rw], sig="qV")
        mk = P.emit("scalar", lambda e, xap=xap, hsl=hsl, ssl=ssl, hts=hts: e.activation(
            out=hts[hsl][:], in_=xap, func=AF.Identity,
            bias=B.stats[:, 58 + ssl:59 + ssl], scale=B.rr(ssl)),
               wait=[mn] + rh.pw(), sig="qA")
        rh.pd(mk)
        rx.release(mk)
        cwh = rh.cw()
        hap = hts[rh.csl()]
        for pk in range(4):
            tpw = B.r_tp.pw()
            slot = B.r_tp.psl()
            mk = None
            for j in range(4):
                c = 4 * pk + j
                mk = P.emit("tensor", lambda e, c=c, j=j, slot=slot, hap=hap: e.matmul(
                    out=B.tp2[slot][:, 128 * j:128 * (j + 1)], lhsT=hap[:, 128 * c:128 * (c + 1)],
                    rhs=B.identbf[:], start=(j == 0), stop=(j == 3)),
                       wait=(cwh if pk == 0 and j == 0 else []) + (tpw if j == 0 else []),
                       sig=("qT" if j == 3 else None))
            B.r_tp.pd(mk)
            if pk == 3:
                rh.release(mk)
            mke = P.emit("scalar", lambda e, pk=pk, tt=tt, slot=slot, hfm=hfm: e.activation(
                out=hfm[:, 4 * pk:4 * pk + 4, 128 * tt:128 * (tt + 1)],
                in_=B.tp2[slot].rearrange("p (j t) -> p j t", j=4),
                func=AF.Identity, bias=0.0, scale=1.0),
                   wait=B.r_tp.cw(), sig="qA")
            B.r_tp.release(mke)


def emit_qkv(P, B, T, layer, seq):
    W = B.WORK
    wsl = [B.bfv(W, 8192).rearrange("p (c o) -> p c o", c=16),
           B.bfv(W + 16384, 8192).rearrange("p (c o) -> p c o", c=16)]
    qst = [B.bfv(W + 32768, 2048), B.bfv(W + 36864, 2048)]
    vst = [B.bfv(W + 40960, 2048), B.bfv(W + 45056, 2048)]
    hfm = B.hfm3()
    kfm = B.kfm3()
    rw = B.r_w2
    rq = B.r_qst
    rv = B.r_vst
    n_win, opw = (12, 4) if layer == 0 else (48, 1)
    for w in range(n_win):
        slw = rw.psl()
        if layer == 0:
            src = T["s_qkvT"].ap()[w].rearrange("p (c o) -> p c o", o=512)
            mk = P.emit("sync", lambda e, src=src, slw=slw, wsl=wsl: e.dma_start(
                out=wsl[slw][:, :, 0:512], in_=src),
                   wait=rw.pw() + (T["prep_marks"]["s_qkvT"] if w == 0 else []),
                   sig=f"dw{slw}")
        else:
            src = T["s_l1qT"].ap()[w].rearrange("p (c o) -> p c o", o=128)
            mk = P.emit("sync", lambda e, src=src, slw=slw, wsl=wsl: e.dma_start(
                out=wsl[slw][:, :, 0:128], in_=src),
                   wait=rw.pw() + (T["prep_marks"]["s_l1qT"] if w == 0 else []),
                   sig=f"dw{slw}")
        rw.pd(mk)
        cww = rw.cw()
        first_mm_of_slab = True
        for ocl in range(opw):
            och = w * opw + ocl
            typ, hh = och // 16, och % 16
            for tokg in range(4):
                pw = B.r_ps6.pw()
                slot = B.r_ps6.psl()
                pap = B.acc6[slot]
                last_mm = None
                for cch in range(NCH):
                    last_mm = P.emit("tensor", lambda e, pap=pap, slw=slw, cch=cch, ocl=ocl, tokg=tokg, wsl=wsl, hfm=hfm: e.matmul(
                        out=pap, lhsT=wsl[slw][:, cch, 128 * ocl:128 * (ocl + 1)],
                        rhs=hfm[:, cch, 512 * tokg:512 * (tokg + 1)],
                        start=(cch == 0), stop=(cch == NCH - 1)),
                           wait=(cww if first_mm_of_slab else []) + (pw if cch == 0 else []),
                           sig=("qT" if cch == NCH - 1 else None))
                    first_mm_of_slab = False
                B.r_ps6.pd(last_mm)
                if ocl == opw - 1 and tokg == 3:
                    rw.release(last_mm)
                cwp = B.r_ps6.cw()
                bias_ap = B.biasq[:, 48 * layer + och:48 * layer + och + 1]
                if typ == 1:
                    mk = P.emit("vector", lambda e, pap=pap, hh=hh, tokg=tokg, bias_ap=bias_ap, kfm=kfm: e.tensor_scalar(
                        out=kfm[:, hh, 512 * tokg:512 * (tokg + 1)], in0=pap, scalar1=bias_ap,
                        scalar2=None, op0=ALU.add),
                           wait=cwp, sig="qV")
                    B.r_ps6.release(mk)
                else:
                    rr = rq if typ == 0 else rv
                    stg = qst if typ == 0 else vst
                    sls = rr.psl()
                    fn = lambda e, pap=pap, sls=sls, tokg=tokg, bias_ap=bias_ap, stg=stg: e.activation(
                        out=stg[sls][:, 512 * tokg:512 * (tokg + 1)], in_=pap, func=AF.Identity,
                        bias=bias_ap, scale=1.0)
                    mk = P.emit("scalar", fn, wait=cwp + (rr.pw() if tokg == 0 else []), sig="qA")
                    B.r_ps6.release(mk)
                    if tokg == 3:
                        rr.pd(mk)
                        scr = T["s_q"] if typ == 0 else T["s_v"]
                        dsem = "dsq" if typ == 0 else "dsv"
                        mk2 = P.emit("sync", lambda e, scr=scr, hh=hh, sls=sls, stg=stg: e.dma_start(
                            out=scr.ap()[hh], in_=stg[sls][:]),
                               wait=rr.cw(), sig=f"{dsem}{sls}")
                        rr.release(mk2)


def emit_attn(P, B, T, layer, seq):

    W = B.WORK
    scsb = [B.f32v(W, 2048), B.f32v(W + 8192, 2048)]
    psbs = [B.bfv(W + 16384 + 4096 * i, 2048) for i in range(4)]
    pt3 = B.bfv(W + 32768, 8192).rearrange("p (k q) -> p k q", q=512)
    qts = [B.bfv(W + 49152, 512), B.bfv(W + 50176, 512)]
    osts = [B.bfv(W + 51200, 512), B.bfv(W + 52224, 512)]
    Daps = [B.bfv(W + 53248 + 256 * i, 128) for i in range(4)]
    vstage = [B.bfv(W + 54528, 2048), B.bfv(W + 58624, 2048)]
    kfm = B.kfm3()
    vtm = B.vtm()

    rvl = B.r_vld
    for h in range(H):
        slv = rvl.psl()
        mk = P.emit("sync", lambda e, h=h, slv=slv, vstage=vstage: e.dma_start(
            out=vstage[slv][:], in_=T["s_v"].ap()[h]),
               wait=rvl.pw() + ([("dsv0", P.cnt("dsv0")), ("dsv1", P.cnt("dsv1"))] if h == 0 else []), sig=f"dv{slv}")
        rvl.pd(mk)
        cwv = rvl.cw()
        vap = vstage[rvl.csl()]
        for pk in range(4):
            tpw = B.r_tp.pw()
            slot = B.r_tp.psl()
            mk = None
            for j in range(4):
                kc = 4 * pk + j
                mk = P.emit("tensor", lambda e, vap=vap, kc=kc, j=j, slot=slot: e.matmul(
                    out=B.tp2[slot][:, 128 * j:128 * (j + 1)], lhsT=vap[:, 128 * kc:128 * (kc + 1)],
                    rhs=B.identbf[:], start=(j == 0), stop=(j == 3)),
                       wait=(cwv if pk == 0 and j == 0 else []) + (tpw if j == 0 else []),
                       sig=("qT" if j == 3 else None))
            B.r_tp.pd(mk)
            if pk == 3:
                rvl.release(mk)
            mke = P.emit("scalar", lambda e, pk=pk, h=h, slot=slot, vtm=vtm: e.activation(
                out=vtm[:, 4 * pk:4 * pk + 4, h, :],
                in_=B.tp2[slot].rearrange("p (j t) -> p j t", j=4),
                func=AF.Identity, bias=0.0, scale=1.0),
                   wait=B.r_tp.cw(), sig="qA")
            B.r_tp.release(mke)

    rqt = B.r_qt
    rsc = B.r_sc
    rpb = B.r_pb
    rD = B.r_D
    rot = B.r_ost
    B._prev_pv = None
    for g in range(4):
        for h in range(H):
            slq = rqt.psl()
            mk = P.emit("sync", lambda e, h=h, g=g, slq=slq, qts=qts: e.dma_start(
                out=qts[slq][:], in_=T["s_q"].ap()[h][:, 512 * g:512 * (g + 1)]),
                   wait=rqt.pw() + ([("dsq0", P.cnt("dsq0")), ("dsq1", P.cnt("dsq1"))] if (g == 0 and h == 0) else []),
                   sig=f"dq{slq}")
            rqt.pd(mk)
            cwq = rqt.cw()
            qap = qts[rqt.csl()]
            tlinfo = []
            prev_mmax = []
            for tl in range(4):
                sls = rsc.psl()
                scap = scsb[sls]
                rsc_pw = rsc.pw() + prev_mmax
                ev_mk = None
                for j in range(g + 1):
                    spw = B.r_sc4.pw()
                    slot = B.r_sc4.psl()
                    pap = B.acc6[slot]
                    mm = P.emit("tensor", lambda e, pap=pap, qap=qap, tl=tl, h=h, j=j, kfm=kfm: e.matmul(
                        out=pap, lhsT=qap[:, 128 * tl:128 * (tl + 1)],
                        rhs=kfm[:, h, 512 * j:512 * (j + 1)], start=True, stop=True),
                           wait=(cwq if (tl == 0 and j == 0) else []) + spw, sig="qT")
                    B.r_sc4.pd(mm)
                    if tl == 3 and j == g:
                        rqt.release(mm)
                    cwp = B.r_sc4.cw()
                    if j < g:
                        ev_mk = P.emit("vector", lambda e, pap=pap, scap=scap, j=j: e.tensor_scalar(
                            out=scap[:, 512 * j:512 * (j + 1)], in0=pap, scalar1=1.0,
                            scalar2=NEGB, op0=ALU.mult, op1=ALU.max,
                            accum_out=B.gmax()[:, j:j + 1]),
                               wait=cwp + (rsc_pw if j == 0 else []), sig="qV")
                    else:
                        mtt = P.emit("vector", lambda e, pap=pap, scap=scap, j=j, tl=tl: e.tensor_tensor(
                            out=scap[:, 512 * j:512 * (j + 1)], in0=pap, in1=B.mask(tl), op=ALU.add),
                               wait=cwp + (rsc_pw if j == 0 else []), sig="qV")
                        ev_mk = P.emit("vector", lambda e, scap=scap, j=j: e.reduce_max(
                            out=B.gmax()[:, j:j + 1], in_=scap[:, 512 * j:512 * (j + 1)], axis=AX.X),
                               wait=[mtt], sig="qV")
                    B.r_sc4.release(ev_mk)
                rsc.pd(ev_mk)
                cws = rsc.cw()
                mmax = P.emit("vector", lambda e, g=g, tl=tl: e.reduce_max(
                    out=B.tiny[:, tl:tl + 1], in_=B.gmax()[:, 0:g + 1], axis=AX.X, negate=True),
                       wait=[ev_mk], sig="qV")
                prev_mmax = [mmax]
                pbw = rpb.pw()
                psl = rpb.psl()
                mexp = P.emit("scalar", lambda e, scap=scap, g=g, tl=tl, psl=psl, psbs=psbs: e.activation(
                    out=psbs[psl][0:128, 0:512 * (g + 1)], in_=scap[:, 0:512 * (g + 1)], func=AF.Exp,
                    bias=B.tiny[:, tl:tl + 1], scale=1.0,
                    accum_out=B.tiny[:, 4 + tl:5 + tl]),
                       wait=[mmax] + pbw + cws, sig="qA")
                rpb.pd(mexp)
                rsc.release(mexp)
                tlinfo.append([mmax, psl, None, mexp])
            for tl in range(4):
                mexp = tlinfo[tl][3]
                mrec = P.emit("vector", lambda e, tl=tl: e.reciprocal(
                    out=B.tiny[:, 8 + tl:9 + tl], in_=B.tiny[:, 4 + tl:5 + tl]),
                       wait=[mexp], sig="qV")
                rdw = rD.pw()
                dsl = rD.psl()
                mD = P.emit("vector", lambda e, tl=tl, dsl=dsl, Daps=Daps: e.tensor_scalar(
                    out=Daps[dsl], in0=B.identbf[:], scalar1=B.tiny[:, 8 + tl:9 + tl],
                    scalar2=None, op0=ALU.mult),
                       wait=[mrec] + rdw, sig="qV")
                rD.pd(mD)
                tlinfo[tl][2] = (dsl, mD)
            nkc = 4 * (g + 1)
            prev_pv = getattr(B, "_prev_pv", None)
            ptw = [dict() for _ in range(nkc // 4)]
            for tl in range(4):
                _, psl, (dsl, mD), _ = tlinfo[tl]
                cwpb = rpb.cw()
                for pk in range(nkc // 4):
                    tpw = B.r_tp.pw()
                    slot = B.r_tp.psl()
                    mk = None
                    for j in range(4):
                        kc = 4 * pk + j
                        mk = P.emit("tensor", lambda e, slot=slot, kc=kc, j=j, psl=psl, dsl=dsl, psbs=psbs, Daps=Daps: e.matmul(
                            out=B.tp2[slot][:, 128 * j:128 * (j + 1)],
                            lhsT=psbs[psl][0:128, 128 * kc:128 * (kc + 1)],
                            rhs=Daps[dsl], start=(j == 0), stop=(j == 3)),
                               wait=((cwpb + [mD]) if (pk == 0 and j == 0) else []) + (tpw if j == 0 else []),
                               sig=("qT" if j == 3 else None))
                    B.r_tp.pd(mk)
                    if pk == nkc // 4 - 1:
                        rpb.release(mk)
                        rD.release(mk)
                    eng = "qA" if (pk + tl) % 2 == 0 else "qV"
                    war = []
                    if prev_pv is not None and eng not in ptw[pk]:
                        war = [prev_pv[min(4 * pk + 3, len(prev_pv) - 1)]]
                    if eng == "qA":
                        mke = P.emit("scalar", lambda e, pk=pk, tl=tl, slot=slot, pt3=pt3: e.activation(
                            out=pt3[:, 4 * pk:4 * pk + 4, 128 * tl:128 * (tl + 1)],
                            in_=B.tp2[slot].rearrange("p (j t) -> p j t", j=4),
                            func=AF.Identity, bias=0.0, scale=1.0),
                               wait=B.r_tp.cw() + war, sig="qA")
                    else:
                        mke = P.emit("vector", lambda e, pk=pk, tl=tl, slot=slot, pt3=pt3: e.tensor_copy(
                            out=pt3[:, 4 * pk:4 * pk + 4, 128 * tl:128 * (tl + 1)],
                            in_=B.tp2[slot].rearrange("p (j t) -> p j t", j=4)),
                               wait=B.r_tp.cw() + war, sig="qV")
                    ptw[pk][eng] = mke
                    B.r_tp.release(mke)
            obw = B.r_ob.pw()
            oslot = B.r_ob.psl()
            opap = [B.pb[:], B.pc[:]][oslot]
            pv_marks = []
            for kc in range(nkc):
                w = (list(ptw[kc // 4].values()) if kc % 4 == 0 else []) + (obw if kc == 0 else [])
                mpv = P.emit("tensor", lambda e, opap=opap, kc=kc, h=h, vtm=vtm, pt3=pt3, nkc=nkc: e.matmul(
                    out=opap, lhsT=vtm[:, kc, h, :], rhs=pt3[:, kc, :],
                    start=(kc == 0), stop=(kc == nkc - 1)),
                       wait=w, sig="qT")
                pv_marks.append(mpv)
            B._prev_pv = pv_marks
            B.r_ob.pd(mpv)
            cwo = B.r_ob.cw()
            oslr = rot.psl()
            mk = P.emit("scalar", lambda e, opap=opap, oslr=oslr, osts=osts: e.activation(
                out=osts[oslr][:], in_=opap, func=AF.Identity, bias=0.0, scale=1.0),
                   wait=cwo + rot.pw(), sig="qA")
            rot.pd(mk)
            B.r_ob.release(mk)
            dst_o = T["s_o"].ap()[g, :, :, h, :].rearrange("tq p q -> p tq q")
            mk2 = P.emit("sync", lambda e, dst_o=dst_o, oslr=oslr, osts=osts: e.dma_start(
                out=dst_o, in_=osts[oslr][:].rearrange("p (tq q) -> p tq q", q=128)),
                   wait=rot.cw(), sig=f"dso{oslr}")
            rot.release(mk2)


def emit_update(P, B, T, layer, seq, src_stream, dst_stream, wait_src=()):
    W = B.WORK
    ots = [B.bfv(W, 2048).rearrange("p (c t) -> p c t", c=16),
           B.bfv(W + 4096, 2048).rearrange("p (c t) -> p c t", c=16)]
    xts = [B.f32v(W + 8192, 2048), B.f32v(W + 16384, 2048)]
    t2s = [B.f32v(W + 24576, 2048), B.f32v(W + 32768, 2048)]
    wslab = B.bfv(B.KFM, 32768).rearrange("p (j c o) -> p j c o", j=4, o=512)
    rot = B.r_ot
    rxt = B.r_xt
    rt2 = B.r_t2

    wname = "s_woutT" if layer == 0 else "s_l1woT"
    for j in range(4):
        wmark = P.emit("sync", lambda e, j=j, wslab=wslab, wname=wname: e.dma_start(
            out=wslab[:, j, :, :].rearrange("p c o -> p (c o)"), in_=T[wname].ap()[j]),
               wait=(T["prep_marks"][wname] if j == 0 else []), sig="dw0")

    for tt in range(NT):
        slo = rot.psl()
        g, tq = tt // 4, tt % 4
        src_o = T["s_o"].ap()[g, tq]
        mko = P.emit("sync", lambda e, src_o=src_o, slo=slo, ots=ots: e.dma_start(
            out=ots[slo][:, :, :], in_=src_o),
               wait=rot.pw() + ([("dso0", P.cnt("dso0")), ("dso1", P.cnt("dso1"))] if tt == 0 else []), sig=f"do{slo}")
        rot.pd(mko)
        slx = rxt.psl()
        mkx = P.emit("sync", lambda e, tt=tt, slx=slx, xts=xts, src_stream=src_stream: e.dma_start(
            out=xts[slx][:], in_=src_stream[128 * tt:128 * (tt + 1), :]),
               wait=rxt.pw() + (list(wait_src) if tt == 0 else []), sig=f"dx{slx}")
        rxt.pd(mkx)
        cwo = rot.cw()
        oap = ots[rot.csl()]
        yslots = [[B.pa[:, 512 * og:512 * (og + 1)] for og in range(4)],
                  [B.pb[:], B.pc[:], B.pd[:], B.pe[:]]]
        yw = B.r_y.pw()
        ysl = B.r_y.psl()
        yaps = yslots[ysl]
        mmy = None
        for og in range(4):
            for cch in range(NCH):
                P.emit("tensor", lambda e, og=og, cch=cch, oap=oap, wslab=wslab, yaps=yaps: e.matmul(
                    out=yaps[og], lhsT=oap[:, cch, :],
                    rhs=wslab[:, og, cch, :], start=(cch == 0), stop=False),
                       wait=(cwo + yw + [wmark] if og == 0 and cch == 0 else []))
            mmy = P.emit("tensor", lambda e, og=og, layer=layer, yaps=yaps: e.matmul(
                out=yaps[og], lhsT=B.onesbf[0:1, :],
                rhs=B.brow[0:1, DM * layer + 512 * og:DM * layer + 512 * (og + 1)],
                start=False, stop=True),
                   sig=("qT" if og == 3 else None))
        B.r_y.pd(mmy)
        rot.release(mmy)
        ssl = tt % 2
        rw = emit_stats(P, B, yaps, ssl, wait0=[mmy])
        slt = rt2.psl()
        if ysl == 0:
            mka = P.emit("vector", lambda e, slt=slt, ssl=ssl, t2s=t2s: e.tensor_scalar(
                out=t2s[slt][:], in0=B.pa[:, 0:2048], scalar1=B.mv(ssl)[:, 0:1], scalar2=B.rr(ssl),
                op0=ALU.subtract, op1=ALU.mult),
                   wait=[rw] + rt2.pw(), sig="qV")
        else:
            for og in range(4):
                mka = P.emit("vector", lambda e, slt=slt, ssl=ssl, t2s=t2s, og=og, yaps=yaps: e.tensor_scalar(
                    out=t2s[slt][:, 512 * og:512 * (og + 1)], in0=yaps[og],
                    scalar1=B.mv(ssl)[:, 0:1], scalar2=B.rr(ssl),
                    op0=ALU.subtract, op1=ALU.mult),
                       wait=([rw] + rt2.pw() if og == 0 else []), sig="qV")
        rt2.pd(mka)
        B.r_y.release(mka)
        cwx = rxt.cw()
        xap = xts[rxt.csl()]
        mkadd = P.emit("vector", lambda e, slt=slt, xap=xap, t2s=t2s: e.tensor_tensor(
            out=t2s[slt][:], in0=t2s[slt][:], in1=xap, op=ALU.add),
               wait=cwx + [mka], sig="qV")
        rxt.release(mkadd)
        mks = P.emit("sync", lambda e, tt=tt, slt=slt, t2s=t2s, dst_stream=dst_stream: e.dma_start(
            out=dst_stream[128 * tt:128 * (tt + 1), :], in_=t2s[slt][:]),
               wait=[mkadd] + rt2.cw(), sig=f"dst{slt}")
        rt2.release(mks)


def emit_f1(P, B, T, layer, seq):
    W = B.WORK
    mats = ["s_gateT", "s_upT"] if layer == 0 else ["s_l1upT"]
    nm = len(mats)
    if layer == 0:
        wsl = [[B.bfv(B.KFM + (2 * r + m) * 16384, 8192).rearrange("p (c i) -> p c i", c=16)
                for m in range(nm)] for r in range(2)]
    else:
        wsl = [[B.bfv(B.KFM + r * 16384, 8192).rearrange("p (c i) -> p c i", c=16)] for r in range(2)]
    sgs = [B.bfv(W, 512), B.bfv(W + 1024, 512)]
    uss = [B.bfv(W + 2048, 512), B.bfv(W + 3072, 512)]
    tst = [B.bfv(W + 4096, 2048), B.bfv(W + 8192, 2048)]
    hfm = B.hfm3()
    rw = B.r_w2
    rsg = B.r_sg
    rus = B.r_us
    rts = B.r_ts
    for iw in range(16):
        slw = rw.psl()
        mk = None
        for m, mn in enumerate(mats):
            src = T[mn].ap()[iw].rearrange("p (c i) -> p c i", i=512)
            mk = P.emit("sync", lambda e, src=src, slw=slw, m=m, wsl=wsl: e.dma_start(
                out=wsl[slw][m][:, :, :], in_=src),
                   wait=(rw.pw() if m == 0 else [])
                        + (T["prep_marks"][mn] if iw == 0 else []),
                   sig=f"dw{slw}")
        rw.pd(mk)
        cww = rw.cw()
        first = True
        for icl in range(4):
            ich = 4 * iw + icl
            slt = rts.psl()
            mlast = None
            for tokg in range(4):
                groups = []
                for m in range(nm):
                    pw = B.r_ps6.pw()
                    slot = B.r_ps6.psl()
                    pap = B.acc6[slot]
                    mm = None
                    for cch in range(NCH):
                        mm = P.emit("tensor", lambda e, pap=pap, slw=slw, m=m, cch=cch, icl=icl, tokg=tokg, wsl=wsl, hfm=hfm: e.matmul(
                            out=pap, lhsT=wsl[slw][m][:, cch, 128 * icl:128 * (icl + 1)],
                            rhs=hfm[:, cch, 512 * tokg:512 * (tokg + 1)],
                            start=(cch == 0), stop=(cch == NCH - 1)),
                               wait=(cww if first else []) + (pw if cch == 0 else []),
                               sig=("qT" if cch == NCH - 1 else None))
                        first = False
                    B.r_ps6.pd(mm)
                    if icl == 3 and tokg == 3 and m == nm - 1:
                        rw.release(mm)
                    groups.append((B.r_ps6.cw(), pap))
                if layer == 0:
                    (cwg, gap), (cwu, uap) = groups
                    slg = rsg.psl()
                    mg = P.emit("scalar", lambda e, gap=gap, slg=slg, sgs=sgs: e.activation(
                        out=sgs[slg][:], in_=gap, func=AF.Silu, bias=0.0, scale=1.0),
                           wait=cwg + rsg.pw(), sig="qA")
                    rsg.pd(mg)
                    B.r_ps6.release(mg)
                    slu = rus.psl()
                    mu = P.emit("vector", lambda e, uap=uap, slu=slu, uss=uss: e.tensor_copy(
                        out=uss[slu][:], in_=uap),
                           wait=cwu + rus.pw(), sig="qV")
                    rus.pd(mu)
                    B.r_ps6.release(mu)
                    mlast = P.emit("vector", lambda e, slg=slg, slu=slu, slt=slt, tokg=tokg, sgs=sgs, uss=uss, tst=tst: e.tensor_tensor(
                        out=tst[slt][:, 512 * tokg:512 * (tokg + 1)], in0=sgs[slg][:],
                        in1=uss[slu][:], op=ALU.mult),
                           wait=rsg.cw() + rus.cw() + (rts.pw() if tokg == 0 else []), sig="qV")
                    rsg.release(mlast)
                    rus.release(mlast)
                else:
                    (cwu, uap) = groups[0]
                    mlast = P.emit("scalar", lambda e, uap=uap, slt=slt, tokg=tokg, tst=tst: e.activation(
                        out=tst[slt][:, 512 * tokg:512 * (tokg + 1)], in_=uap, func=AF.Relu,
                        bias=0.0, scale=1.0),
                           wait=cwu + (rts.pw() if tokg == 0 else []), sig="qA")
                    B.r_ps6.release(mlast)
            rts.pd(mlast)
            mk2 = P.emit("gpsimd", lambda e, ich=ich, slt=slt, tst=tst: e.dma_start(
                out=T["s_t"].ap()[ich], in_=tst[slt][:]),
                   wait=rts.cw(), sig=f"dstt{slt}")
            rts.release(mk2)


def emit_f2(P, B, T, layer, seq, src_stream, dst_stream, wait_src=()):
    W = B.WORK
    xts = [B.f32v(W, 2048), B.f32v(W + 8192, 2048)]
    t2s = [B.f32v(W + 16384, 2048)]
    ysb = [B.f32v(W + 24576, 512), B.f32v(W + 26624, 512)]
    wdn = "s_downT" if layer == 0 else "s_l1downT"
    wds = [B.bfv(B.KFM + r * 16384, 8192).rearrange("p (i o) -> p i o", i=64) for r in range(2)]
    ytm = B.f32v(B.KFM + 32768, 8192).rearrange("p (t o) -> p t o", t=4)
    treg = B.treg()
    rtr = B.r_tr
    rwd = B.r_wd
    rys = B.r_ys
    rytm = B.r_ytm
    rxt = B.r_xt
    rt2 = B.r_t2f
    for tr in range(4):
        src_t = T["s_t"].ap()[:, :, 512 * tr:512 * (tr + 1)].rearrange("i p t -> p i t")
        mtr = P.emit("sync", lambda e, src_t=src_t, treg=treg: e.dma_start(out=treg[:, :, :], in_=src_t),
               wait=rtr.pw() + ([("dstt0", P.cnt("dstt0")), ("dstt1", P.cnt("dstt1"))] if tr == 0 else []), sig="dt")
        rtr.pd(mtr)
        cwt = rtr.cw()
        ytw = rytm.pw()
        mke = None
        for och in range(16):
            slw = rwd.psl()
            src_w = T[wdn].ap()[och].rearrange("p (i o) -> p i o", o=128)
            mkw = P.emit("sync", lambda e, src_w=src_w, slw=slw, wds=wds: e.dma_start(
                out=wds[slw][:, :, :], in_=src_w),
                   wait=rwd.pw() + (T["prep_marks"][wdn] if (tr == 0 and och == 0) else []),
                   sig=f"dw{slw}")
            rwd.pd(mkw)
            cww = rwd.cw()
            obw = B.r_ob.pw()
            oslot = B.r_ob.psl()
            opap = [B.pb[:], B.pc[:]][oslot]
            mdown = None
            for ich in range(NICH):
                mdown = P.emit("tensor", lambda e, opap=opap, slw=slw, ich=ich, wds=wds, treg=treg: e.matmul(
                    out=opap, lhsT=wds[slw][:, ich, :], rhs=treg[:, ich, :],
                    start=(ich == 0), stop=(ich == NICH - 1)),
                       wait=(((cwt if och == 0 else []) + cww + obw) if ich == 0 else []),
                       sig=("qT" if ich == NICH - 1 else None))
            B.r_ob.pd(mdown)
            rwd.release(mdown)
            if och == 15:
                rtr.release(mdown)
            cwo = B.r_ob.cw()
            sly = rys.psl()
            mys = P.emit("scalar", lambda e, opap=opap, sly=sly, ysb=ysb: e.copy(
                out=ysb[sly][:], in_=opap),
                   wait=cwo + rys.pw(), sig="qA")
            rys.pd(mys)
            B.r_ob.release(mys)
            cwy = rys.cw()
            yap = ysb[rys.csl()]
            tpw = B.r_tp.pw()
            slot = B.r_tp.psl()
            mk = None
            for j in range(4):
                mk = P.emit("tensor", lambda e, slot=slot, yap=yap, j=j: e.matmul(
                    out=B.tp2[slot][:, 128 * j:128 * (j + 1)], lhsT=yap[:, 128 * j:128 * (j + 1)],
                    rhs=B.identf[:], start=(j == 0), stop=(j == 3)),
                       wait=((cwy + tpw) if j == 0 else []),
                       sig=("qT" if j == 3 else None))
            B.r_tp.pd(mk)
            rys.release(mk)
            mke = P.emit("vector", lambda e, och=och, slot=slot, ytm=ytm: e.tensor_copy(
                out=ytm[:, 0:4, 128 * och:128 * (och + 1)],
                in_=B.tp2[slot].rearrange("p (j t) -> p j t", j=4)),
                   wait=B.r_tp.cw() + (ytw if och == 0 else []), sig="qV")
            B.r_tp.release(mke)
        rytm.pd(mke)
        cwytm = rytm.cw()
        mlast_apply = None
        for ttl in range(4):
            tt = 4 * tr + ttl
            slx = rxt.psl()
            mkx = P.emit("sync", lambda e, tt=tt, slx=slx, xts=xts, src_stream=src_stream: e.dma_start(
                out=xts[slx][:], in_=src_stream[128 * tt:128 * (tt + 1), :]),
                   wait=rxt.pw() + (list(wait_src) if tt == 0 else []), sig=f"dx{slx}")
            rxt.pd(mkx)
            ssl = ttl % 2
            yrow = ytm[:, ttl, :]
            rw = emit_stats(P, B, [yrow[:, 512 * gg:512 * (gg + 1)] for gg in range(4)], ssl,
                            wait0=(cwytm if ttl == 0 else []))
            mka = P.emit("vector", lambda e, yrow=yrow, ssl=ssl, t2s=t2s: e.tensor_scalar(
                out=t2s[0][:], in0=yrow, scalar1=B.mv(ssl)[:, 0:1], scalar2=B.rr(ssl),
                op0=ALU.subtract, op1=ALU.mult),
                   wait=[rw] + rt2.pw(), sig="qV")
            rt2.pd(mka)
            mlast_apply = mka
            cwx = rxt.cw()
            xap = xts[rxt.csl()]
            mkadd = P.emit("vector", lambda e, xap=xap, t2s=t2s: e.tensor_tensor(
                out=t2s[0][:], in0=t2s[0][:], in1=xap, op=ALU.add),
                   wait=cwx + [mka], sig="qV")
            rxt.release(mkadd)
            mks = P.emit("sync", lambda e, tt=tt, t2s=t2s, dst_stream=dst_stream: e.dma_start(
                out=dst_stream[128 * tt:128 * (tt + 1), :], in_=t2s[0][:]),
                   wait=[mkadd] + rt2.cw(), sig="dstf")
            rt2.release(mks)
        rytm.release(mlast_apply)


def emit_prep(P, B, T, which):
    W = B.WORK
    pins = [B.f32v(W, 8192), B.f32v(W + 32768, 8192)]
    pini = [B.i32v(W, 8192), B.i32v(W + 32768, 8192)]
    pbfs = [B.bfv(B.KFM, 8192), B.bfv(B.KFM + 16384, 8192)]
    stgs = [B.bfv(B.KFM + 32768, 8192), B.bfv(B.KFM + 49152, 8192)]
    rpin = B.r_pin
    rpbf = B.r_pbf
    rstg = B.r_stg

    sc_specs = [("L0_ffn_gate_scale", 64, 0), ("L0_ffn_up_scale", 64, 64),
                ("L0_ffn_down_scale", 16, 128), ("L1_ffn_up_scale", 64, 144),
                ("L1_ffn_down_scale", 16, 208)]
    tmp = B.f32v(W, 5 * 128).rearrange("p (a b) -> p a b", b=128)
    for idx, (nm, ncol, base) in enumerate(sc_specs):
        src = T[nm].ap().rearrange("(c p) -> c p", p=128)
        P.emit("sync", lambda e, src=src, ncol=ncol, idx=idx, tmp=tmp: e.dma_start(
            out=tmp[0:ncol, idx, :], in_=src), sig="dsc")
    P.emit("sync", lambda e: e.dma_start(out=B.tiny[0:1, 6:7],
                                         in_=T["L1_ffn_output_scale"].ap().rearrange("(o a) -> o a", o=1)),
           sig="dsc")
    ld_all = [("dsc", P.cnt("dsc"))]
    mev = None
    for idx, (nm, ncol, base) in enumerate(sc_specs):
        mm = P.emit("tensor", lambda e, ncol=ncol, idx=idx, tmp=tmp: e.matmul(
            out=B.pd[:, 0:ncol], lhsT=tmp[0:ncol, idx, :],
            rhs=B.identf[0:ncol, 0:ncol], start=True, stop=True),
               wait=ld_all + ([mev] if mev else []), sig="qT")
        mev = P.emit("vector", lambda e, ncol=ncol, base=base: e.tensor_copy(
            out=B.scT[:, base:base + ncol], in_=B.pd[:, 0:ncol]),
               wait=[mm], sig="qV")
    mm = P.emit("tensor", lambda e: e.matmul(out=B.pd[:, 0:1], lhsT=B.onesf[0:1, :], rhs=B.tiny[0:1, 6:7],
                                             start=True, stop=True),
           wait=ld_all + [mev], sig="qT")
    mev = P.emit("vector", lambda e: e.tensor_copy(out=B.osb[:], in_=B.pd[:, 0:1]),
           wait=[mm], sig="qV")
    mos = P.emit("vector", lambda e: e.tensor_scalar(out=B.scT[:, 208:224], in0=B.scT[:, 208:224],
                                                     scalar1=B.osb[:], scalar2=None, op0=ALU.mult),
           wait=[mev], sig="qV")
    scale_ready = [mos]
    tp_gate = [mos]

    specs = {
        "s_qkvT": ("L0_attn_Wqkv", "cast", 6144, 2048, None),
        "s_woutT": ("L0_attn_Wout", "cast", 2048, 2048, None),
        "s_gateT": ("L0_ffn_gate_w", "dequant", 8192, 2048, 0),
        "s_upT": ("L0_ffn_up_w", "dequant", 8192, 2048, 64),
        "s_downT": ("L0_ffn_down_w", "dequant", 2048, 8192, 128),
        "s_l1upT": ("L1_ffn_up_weight", "sign", 8192, 2048, 144),
        "s_l1downT": ("L1_ffn_down_weight", "sign", 2048, 8192, 208),
        "s_l1qT": (None, "l1cast", None, None, None),
        "s_l1woT": (None, "l1cast", None, None, None),
    }

    def emit_l1cast(dn):
        dst = T[dn].ap()
        jobs = []
        if dn == "s_l1qT":
            for m, mn in enumerate(["L1_attn_Wq", "L1_attn_Wk", "L1_attn_Wv"]):
                for h in range(H):
                    src = T[mn].ap()[h].rearrange("(cc p) d -> p cc d", p=128)
                    jobs.append((src, dst[m * 16 + h].rearrange("p (c d) -> p c d", d=128)))
        else:
            for cch in range(16):
                src = T["L1_attn_Wo"].ap()[128 * cch:128 * (cch + 1), :]
                jobs.append((src, dst[:, :, 512 * cch:512 * (cch + 1)]))
        for ji, (src, dap) in enumerate(jobs):
            slb = rpbf.psl()
            nel = 2048
            mk = P.emit("gpsimd", lambda e, src=src, slb=slb, nel=nel: e.dma_start(
                out=pbfs[slb][0:128, 0:nel], in_=src),
                   wait=rpbf.pw(), sig=f"dp{slb}")
            if dn == "s_l1qT" and ji < 16:
                mk = P.emit("vector", lambda e, slb=slb, nel=nel, pbfs=pbfs: e.tensor_scalar(
                    out=pbfs[slb][0:128, 0:nel], in0=pbfs[slb][0:128, 0:nel],
                    scalar1=QSCALE, scalar2=None, op0=ALU.mult),
                       wait=[mk], sig="qV")
            rpbf.pd(mk)
            cwb = rpbf.cw()
            if dn == "s_l1qT":
                sap = pbfs[slb][0:128, 0:nel].rearrange("p (c d) -> p c d", d=128)
                mk2 = P.emit("gpsimd", lambda e, dap=dap, sap=sap: e.dma_start(out=dap, in_=sap),
                       wait=cwb, sig=f"dpl{slb}")
            else:
                for j in range(4):
                    mk2 = P.emit("gpsimd", lambda e, dap=dap, slb=slb, j=j: e.dma_start(
                        out=dap[j], in_=pbfs[slb][0:128, 512 * j:512 * (j + 1)]),
                           wait=(cwb if j == 0 else []), sig=f"dpl{slb}")
            rpbf.release(mk2)
        T["prep_marks"][dn] = [("dpl0", P.cnt("dpl0")), ("dpl1", P.cnt("dpl1"))]

    for dn in which:
        srcn, mode, OUT, IN, scol = specs[dn]
        if mode == "l1cast":
            emit_l1cast(dn)
            continue
        src = T[srcn].ap()
        dst = T[dn].ap()
        for rt in range(OUT // 128):
            sli = rpin.psl()
            if mode == "cast":
                slb = rpbf.psl()
                mk = P.emit("gpsimd", lambda e, src=src, rt=rt, slb=slb, IN=IN, pbfs=pbfs: e.dma_start(
                    out=pbfs[slb][0:128, 0:IN], in_=src[128 * rt:128 * (rt + 1), :]),
                       wait=rpbf.pw() + (scale_ready if rt == 0 else []), sig=f"dp{slb}")
                if dn == "s_qkvT" and rt < 16:
                    mk = P.emit("vector", lambda e, slb=slb, IN=IN, pbfs=pbfs: e.tensor_scalar(
                        out=pbfs[slb][0:128, 0:IN], in0=pbfs[slb][0:128, 0:IN],
                        scalar1=QSCALE, scalar2=None, op0=ALU.mult),
                           wait=[mk], sig="qV")
                rpbf.pd(mk)
            elif mode == "dequant":
                mk = P.emit("gpsimd", lambda e, src=src, rt=rt, sli=sli, IN=IN, pini=pini: e.dma_start(
                    out=pini[sli][0:128, 0:IN], in_=src[128 * rt:128 * (rt + 1), :]),
                       wait=rpin.pw(), sig=f"dpi{sli}")
                rpin.pd(mk)
                cwi = rpin.cw()
                slb = rpbf.psl()
                scap = B.scT[:, scol + rt:scol + rt + 1]
                mk = P.emit("vector", lambda e, sli=sli, slb=slb, IN=IN, scap=scap, pini=pini, pbfs=pbfs: e.tensor_scalar(
                    out=pbfs[slb][0:128, 0:IN], in0=pini[sli][0:128, 0:IN],
                    scalar1=-128.0, scalar2=scap, op0=ALU.add, op1=ALU.mult),
                       wait=cwi + rpbf.pw() + (scale_ready if rt == 0 else []), sig="qV")
                rpbf.pd(mk)
                rpin.release(mk)
            else:
                mdma = P.emit("gpsimd", lambda e, src=src, rt=rt, sli=sli, IN=IN, pins=pins: e.dma_start(
                    out=pins[sli][0:128, 0:IN], in_=src[128 * rt:128 * (rt + 1), :]),
                       wait=rpin.pw() + (scale_ready if rt == 0 else []), sig=f"dpi{sli}")
                rpin.pd(mdma)
                cwi = rpin.cw()
                slb = rpbf.psl()
                scap = B.scT[:, scol + rt:scol + rt + 1]
                msgn = P.emit("scalar", lambda e, sli=sli, slb=slb, IN=IN, pins=pins, pbfs=pbfs: e.activation(
                    out=pbfs[slb][0:128, 0:IN], in_=pins[sli][0:128, 0:IN], func=AF.Sign,
                    bias=0.0, scale=1.0),
                       wait=[mdma] + cwi + rpbf.pw(), sig="qA")
                rpin.release(msgn)
                mk = P.emit("vector", lambda e, slb=slb, IN=IN, scap=scap, pbfs=pbfs: e.tensor_scalar(
                    out=pbfs[slb][0:128, 0:IN], in0=pbfs[slb][0:128, 0:IN],
                    scalar1=scap, scalar2=None, op0=ALU.mult),
                       wait=[msgn] + (scale_ready if rt == 0 else []), sig="qV")
                rpbf.pd(mk)
            cwb = rpbf.cw()
            bap = pbfs[rpbf.csl()]
            wsz = 1 if IN == 8192 else 4
            if rt % wsz == 0:
                sls = rstg.psl()
                first_v = True
                first_a = True
            npk = IN // 512
            mke = None
            for pk in range(npk):
                tpw = B.r_tp.pw()
                slot = B.r_tp.psl()
                mk = None
                for j in range(4):
                    c = 4 * pk + j
                    mk = P.emit("tensor", lambda e, slot=slot, bap=bap, c=c, j=j: e.matmul(
                        out=B.tp2[slot][:, 128 * j:128 * (j + 1)], lhsT=bap[0:128, 128 * c:128 * (c + 1)],
                        rhs=B.identbf[:], start=(j == 0), stop=(j == 3)),
                           wait=(cwb if pk == 0 and j == 0 else []) + (tpw if j == 0 else [])
                                + (tp_gate if (tp_gate and pk == 0 and j == 0) else []),
                           sig=("qT" if j == 3 else None))
                tp_gate = None
                B.r_tp.pd(mk)
                if pk == npk - 1:
                    rpbf.release(mk)
                if wsz == 1:
                    sdst = stgs[sls][:, 512 * pk:512 * (pk + 1)]
                    sr = B.tp2[slot]
                else:
                    sdst = stgs[sls].rearrange("p (c o) -> p c o", o=512)[:, 4 * pk:4 * pk + 4,
                                                                         128 * (rt % 4):128 * (rt % 4) + 128]
                    sr = B.tp2[slot].rearrange("p (j t) -> p j t", j=4)
                if pk % 2 == 0:
                    mke = P.emit("vector", lambda e, sdst=sdst, sr=sr: e.tensor_copy(out=sdst, in_=sr),
                           wait=B.r_tp.cw() + (rstg.pw() if first_v else []), sig="qV")
                    first_v = False
                else:
                    mke = P.emit("scalar", lambda e, sdst=sdst, sr=sr: e.activation(
                        out=sdst, in_=sr, func=AF.Identity, bias=0.0, scale=1.0),
                           wait=B.r_tp.cw() + (rstg.pw() if first_a else []), sig="qA")
                    first_a = False
                B.r_tp.release(mke)
            if rt % wsz == wsz - 1:
                rstg.pd(mke)
                rstg.cw()
                both = [("qV", P.cnt("qV")), ("qA", P.cnt("qA"))]
                dst_ap = dst[rt] if wsz == 1 else dst[rt // 4]
                mk2 = P.emit("gpsimd", lambda e, dst_ap=dst_ap, sls=sls: e.dma_start(
                    out=dst_ap, in_=stgs[sls][:, 0:8192]),
                       wait=both, sig=f"dpo{sls}")
                rstg.release(mk2)
        T["prep_marks"][dn] = [("dpo0", P.cnt("dpo0")), ("dpo1", P.cnt("dpo1"))]


def emit_bias_prep(P, B, T, layer):
    W = B.WORK
    tmp = B.f32v(W, 3 * 128).rearrange("p (a b) -> p a b", b=128)
    dn = f"db{layer}"
    if layer == 0:
        P.emit("sync", lambda e, tmp=tmp: e.dma_start(
            out=tmp[0:48, 0, :], in_=T["L0_attn_bqkv"].ap().rearrange("(c p) -> c p", p=128)),
               sig=dn)
        P.emit("gpsimd", lambda e: e.dma_start(out=B.brow[0:1, 0:DM],
                                               in_=T["L0_attn_bout"].ap().rearrange("(o a) -> o a", o=1)),
               sig=dn + "g")
        ld_all = [(dn, P.cnt(dn)), (dn + "g", P.cnt(dn + "g"))]
        mm = P.emit("tensor", lambda e, tmp=tmp: e.matmul(out=B.pd[:, 0:48], lhsT=tmp[0:48, 0, :],
                                                          rhs=B.identf[0:48, 0:48], start=True, stop=True),
               wait=ld_all, sig="qT")
        mbq = P.emit("vector", lambda e: e.tensor_copy(out=B.biasq[:, 0:48], in_=B.pd[:, 0:48]),
               wait=[mm], sig="qV")
        P.emit("vector", lambda e: e.tensor_scalar(out=B.biasq[:, 0:16], in0=B.biasq[:, 0:16],
                                                   scalar1=QSCALE, scalar2=None, op0=ALU.mult),
               wait=[mbq], sig="qV")
    else:
        for i, nm in enumerate(["L1_attn_bq", "L1_attn_bk", "L1_attn_bv"]):
            P.emit("sync", lambda e, nm=nm, i=i, tmp=tmp: e.dma_start(out=tmp[0:16, i, :], in_=T[nm].ap()),
                   sig=dn)
        P.emit("gpsimd", lambda e: e.dma_start(out=B.brow[0:1, DM:2 * DM],
                                               in_=T["L1_attn_bo"].ap().rearrange("(o a) -> o a", o=1)),
               sig=dn + "g")
        ld_all = [(dn, P.cnt(dn)), (dn + "g", P.cnt(dn + "g"))]
        mev = None
        for i in range(3):
            mm = P.emit("tensor", lambda e, i=i, tmp=tmp: e.matmul(out=B.pd[:, 0:16], lhsT=tmp[0:16, i, :],
                                                                   rhs=B.identf[0:16, 0:16], start=True, stop=True),
                   wait=ld_all + ([mev] if mev else []), sig="qT")
            mev = P.emit("vector", lambda e, i=i: e.tensor_copy(
                out=B.biasq[:, 48 + 16 * i:48 + 16 * (i + 1)], in_=B.pd[:, 0:16]),
                   wait=[mm], sig="qV")
            if i == 0:
                mev = P.emit("vector", lambda e: e.tensor_scalar(
                    out=B.biasq[:, 48:64], in0=B.biasq[:, 48:64],
                    scalar1=QSCALE, scalar2=None, op0=ALU.mult),
                       wait=[mev], sig="qV")


def emit_const_load(P, B, T):
    for nm, dst in [("c_identbf", B.identbf), ("c_identf", B.identf), ("c_onesbf", B.onesbf),
                    ("c_onesf", B.onesf), ("c_masks", B.masks)]:
        P.emit("sync", lambda e, nm=nm, dst=dst: e.dma_start(out=dst[:], in_=T[nm].ap()), sig="dc")
    for e in ENGINES:
        P.emit(e, None, wait=[("dc", P.cnt("dc"))])


INPUT_SPECS = {
    "L0_attn_Wqkv": ([3 * DM, DM], F32), "L0_attn_bqkv": ([3 * DM], F32),
    "L0_attn_Wout": ([DM, DM], F32), "L0_attn_bout": ([DM], F32),
    "L0_ffn_gate_w": ([DFF, DM], I32), "L0_ffn_gate_scale": ([DFF], F32),
    "L0_ffn_up_w": ([DFF, DM], I32), "L0_ffn_up_scale": ([DFF], F32),
    "L0_ffn_down_w": ([DM, DFF], I32), "L0_ffn_down_scale": ([DM], F32),
    "L1_attn_Wq": ([H, DM, HD], F32), "L1_attn_bq": ([H, HD], F32),
    "L1_attn_Wk": ([H, DM, HD], F32), "L1_attn_bk": ([H, HD], F32),
    "L1_attn_Wv": ([H, DM, HD], F32), "L1_attn_bv": ([H, HD], F32),
    "L1_attn_Wo": ([DM, DM], F32), "L1_attn_bo": ([DM], F32),
    "L1_ffn_up_weight": ([DFF, DM], F32), "L1_ffn_down_weight": ([DM, DFF], F32),
    "L1_ffn_up_scale": ([DFF], F32), "L1_ffn_down_scale": ([DM], F32),
    "L1_ffn_output_scale": ([1], F32),
}
CONST_SPECS = {
    "c_identbf": ([128, 128], BF16), "c_identf": ([128, 128], F32),
    "c_onesbf": ([1, 128], BF16), "c_onesf": ([1, 128], F32),
    "c_masks": ([128, 4 * 512], BF16),
}
SCRATCH_SPECS = {
    "s_qkvT": ([12, 128, 16 * 512], BF16),
    "s_woutT": ([4, 128, 16 * 512], BF16),
    "s_gateT": ([16, 128, 16 * 512], BF16),
    "s_upT": ([16, 128, 16 * 512], BF16),
    "s_downT": ([16, 128, 64 * 128], BF16),
    "s_l1qT": ([48, 128, 16 * 128], BF16),
    "s_l1woT": ([4, 128, 16 * 512], BF16),
    "s_l1upT": ([16, 128, 16 * 512], BF16),
    "s_l1downT": ([16, 128, 64 * 128], BF16),
}
LOCAL_SCRATCH = {
    "s_q": ([H, 128, S], BF16), "s_v": ([H, 128, S], BF16),
    "s_o": ([4, 4, 128, H, 128], BF16),
    "s_t": ([NICH, 128, S], BF16),
}


def build_program(cfg):
    nc = bass.Bass("TRN2", target_bir_lowering=False)
    ctx = ExitStack()
    T = {"prep_marks": {}}
    for n in cfg.get("inputs", []):
        sh, dt = INPUT_SPECS[n]
        T[n] = nc.dram_tensor(n, sh, dt, kind="ExternalInput")
    for n in CONST_SPECS:
        sh, dt = CONST_SPECS[n]
        T[n] = nc.dram_tensor(n, sh, dt, kind="ExternalInput")
    for n in cfg.get("scratch_in", []):
        sh, dt = SCRATCH_SPECS[n]
        T[n] = nc.dram_tensor(n, sh, dt, kind="ExternalInput")
        T["prep_marks"][n] = []
    for n in cfg.get("scratch_out", []):
        sh, dt = SCRATCH_SPECS[n]
        T[n] = nc.dram_tensor(n, sh, dt, kind="ExternalOutput")
    for n in cfg.get("scratch_internal", []):
        sh, dt = SCRATCH_SPECS[n]
        T[n] = nc.dram_tensor(n, sh, dt, kind="Internal")
        T["prep_marks"][n] = []
    for n, (sh, dt) in LOCAL_SCRATCH.items():
        T[n] = nc.dram_tensor(n, sh, dt, kind="Internal")
    if cfg.get("x_in"):
        T["x"] = nc.dram_tensor("x", [NSEQ, S, DM], F32, kind="ExternalInput")
    if cfg.get("stream_in"):
        T["stream_i"] = nc.dram_tensor("stream_i", [NSEQ, S, DM], F32, kind="ExternalInput")
    if cfg.get("stream_out"):
        T["stream_o"] = nc.dram_tensor("stream_o", [NSEQ, S, DM], F32, kind="ExternalOutput")
    if cfg.get("out"):
        T["out"] = nc.dram_tensor("out", [NSEQ, S, DM], F32, kind="ExternalOutput")
    if cfg.get("stream_internal"):
        T["stream_n"] = nc.dram_tensor("stream_n", [NSEQ, S, DM], F32, kind="Internal")

    plan = Plan(nc, ctx)
    B = Bufs(nc, ctx, plan)
    for nm, n in [("ps6", 6), ("sc4", 4), ("ob", 2), ("tp", 2), ("ry", 2), ("p1x", 2),
                  ("p1h", 2), ("w2", 2), ("qst", 2), ("vst", 2), ("vld", 2), ("qt", 2),
                  ("sc", 2), ("pbb", 4), ("rd", 4), ("ptb", 1), ("ostr", 2), ("ot", 2), ("xt", 2),
                  ("t2", 2), ("sg", 2), ("us", 2), ("ts", 2), ("tr", 1), ("wd", 2),
                  ("ys", 2), ("ytm", 1), ("t2f", 1), ("pin", 2), ("pbf", 2), ("stg", 2)]:
        setattr(B, "r_" + nm, Ring(plan, nm, n))
    B.r_y = B.r_ry
    B.r_pb = B.r_pbb
    B.r_D = B.r_rd
    B.r_ptb = B.r_ptb
    B.r_ost = B.r_ostr

    emit_const_load(plan, B, T)
    plan.barrier()

    stream_written = {}

    def src_dst(seq, blk_global):
        if cfg.get("x_in") and blk_global == cfg["first_blk"]:
            srcap = T["x"].ap()[seq]
        elif cfg.get("stream_in") and blk_global == cfg["first_blk"]:
            srcap = T["stream_i"].ap()[seq]
        else:
            srcap = T["stream_n"].ap()[seq] if cfg.get("stream_internal") else T["stream_o"].ap()[seq]
        if blk_global == cfg["last_blk"]:
            dstap = T["out"].ap()[seq] if cfg.get("out") else T["stream_o"].ap()[seq]
        else:
            dstap = T["stream_n"].ap()[seq] if cfg.get("stream_internal") else T["stream_o"].ap()[seq]
        return srcap, dstap

    for unit in cfg["units"]:
        if unit[0] == "prep":
            emit_prep(plan, B, T, unit[1])
            plan.barrier()
            continue
        _, blk = unit
        layer, kind = blk // 2, ("attn" if blk % 2 == 0 else "ffn")
        if kind == "attn":
            emit_bias_prep(plan, B, T, layer)
            plan.barrier()
        phs = cfg.get("phases")
        for seq in range(NSEQ):
            srcap, dstap = src_dst(seq, blk)
            wait_src = stream_written.get(seq, [])
            if phs is None or "p1" in phs:
                emit_p1(plan, B, T, srcap, wait_first=wait_src)
                plan.barrier()
            if kind == "attn":
                if phs is None or "qkv" in phs:
                    emit_qkv(plan, B, T, layer, seq)
                    plan.barrier()
                if phs is None or "attn" in phs:
                    emit_attn(plan, B, T, layer, seq)
                    plan.barrier()
                if phs is None or "update" in phs:
                    emit_update(plan, B, T, layer, seq, srcap, dstap, wait_src=wait_src)
                    plan.barrier()
            else:
                if phs is None or "f1" in phs:
                    emit_f1(plan, B, T, layer, seq)
                    plan.barrier()
                if phs is None or "f2" in phs:
                    emit_f2(plan, B, T, layer, seq, srcap, dstap, wait_src=wait_src)
                    plan.barrier()
            stream_written[seq] = [(n_, plan.cnt(n_)) for n_ in ("dst0", "dst1", "dstf")]

    for semn in ("dst0", "dst1", "dstf", "dpo0", "dpo1", "dpl0", "dpl1"):
        plan.emit("sync", None, wait=[(semn, plan.cnt(semn))])

    with ctx:
        with nc.Block() as block:
            def runner(ops):
                def f(e):
                    for op in ops:
                        op(e)
                return f
            block.sync(runner(plan.ops["sync"]))
            block.tensor(runner(plan.ops["tensor"]))
            block.vector(runner(plan.ops["vector"]))
            block.scalar(runner(plan.ops["scalar"]))
            block.gpsimd(runner(plan.ops["gpsimd"]))
    return nc, T


def host_consts():
    identbf = np.eye(128, dtype=BF)
    identf = np.eye(128, dtype=np.float32)
    onesbf = np.ones((1, 128), dtype=BF)
    onesf = np.ones((1, 128), dtype=np.float32)
    masks = np.zeros((128, 4 * 512), dtype=np.float32)
    q = np.arange(128)[:, None]
    for tl in range(4):
        kl = np.arange(512)[None, :]
        masks[:, 512 * tl:512 * (tl + 1)] = np.where(kl <= 128 * tl + q, 0.0, NEGB)
    return {"c_identbf": identbf, "c_identf": identf, "c_onesbf": onesbf,
            "c_onesf": onesf, "c_masks": masks.astype(BF)}


_PROG_CACHE = {}


def _get_prog(key, cfg):
    if key not in _PROG_CACHE:
        _PROG_CACHE[key] = build_program(cfg)
    return _PROG_CACHE[key]


ALL_SCRATCH = list(SCRATCH_SPECS.keys())
PREP_ALL = ("prep", ALL_SCRATCH)

LAUNCH_CFGS = {
    "prep": dict(
        inputs=["L0_attn_Wqkv", "L0_attn_Wout", "L0_ffn_gate_w", "L0_ffn_gate_scale",
                "L0_ffn_up_w", "L0_ffn_up_scale", "L0_ffn_down_w", "L0_ffn_down_scale",
                "L1_attn_Wq", "L1_attn_Wk", "L1_attn_Wv", "L1_attn_Wo",
                "L1_ffn_up_weight", "L1_ffn_down_weight", "L1_ffn_up_scale",
                "L1_ffn_down_scale", "L1_ffn_output_scale"],
        scratch_out=ALL_SCRATCH, units=[PREP_ALL], first_blk=-1, last_blk=-1),
    "b0": dict(inputs=["L0_attn_bqkv", "L0_attn_bout"], scratch_in=["s_qkvT", "s_woutT"],
               x_in=True, stream_out=True, units=[("blk", 0)], first_blk=0, last_blk=0),
    "b1": dict(inputs=[], scratch_in=["s_gateT", "s_upT", "s_downT"],
               stream_in=True, stream_out=True, units=[("blk", 1)], first_blk=1, last_blk=1),
    "b2": dict(inputs=["L1_attn_bq", "L1_attn_bk", "L1_attn_bv", "L1_attn_bo"],
               scratch_in=["s_l1qT", "s_l1woT"],
               stream_in=True, stream_out=True, units=[("blk", 2)], first_blk=2, last_blk=2),
    "b3": dict(inputs=[], scratch_in=["s_l1upT", "s_l1downT"],
               stream_in=True, out=True, units=[("blk", 3)], first_blk=3, last_blk=3),
}

FUSED_CFG = dict(
    inputs=[n for n in INPUT_SPECS],
    scratch_internal=ALL_SCRATCH,
    x_in=True, out=True, stream_internal=True,
    units=[PREP_ALL, ("blk", 0), ("blk", 1), ("blk", 2), ("blk", 3)],
    first_blk=0, last_blk=3)

MODE = "fused"


def kernel(**inputs):
    inputs = {k: np.asarray(v) for k, v in inputs.items()}
    x_full = inputs["L0_attn_x"]
    consts = host_consts()

    def run(name, extra_by_core):
        cfg = FUSED_CFG if name == "fused" else LAUNCH_CFGS[name]
        nc, T = _get_prog(name, cfg)
        in_maps = []
        for c in range(NCORE):
            m = dict(consts)
            for n in cfg.get("inputs", []):
                m[n] = inputs[n]
            for n in cfg.get("scratch_in", []):
                m[n] = extra_by_core[c][n]
            if cfg.get("x_in"):
                m["x"] = extra_by_core[c]["x"]
            if cfg.get("stream_in"):
                m["stream_i"] = extra_by_core[c]["stream_i"]
            in_maps.append(m)
        res = run_bass_kernel_spmd(nc, in_maps, core_ids=list(range(NCORE)))
        return res.results

    xs = [np.ascontiguousarray(x_full[2 * c:2 * c + 2]) for c in range(NCORE)]
    if MODE == "fused":
        r = run("fused", [dict(x=xs[c]) for c in range(NCORE)])
        out = np.concatenate([np.asarray(r[c]["out"]) for c in range(NCORE)], axis=0)
    else:
        prep_out = run("prep", [dict() for _ in range(NCORE)])
        extra = [dict(prep_out[c]) for c in range(NCORE)]
        for c in range(NCORE):
            extra[c]["x"] = xs[c]
        r = run("b0", extra)
        for c in range(NCORE):
            extra[c]["stream_i"] = np.asarray(r[c]["stream_o"])
        r = run("b1", extra)
        for c in range(NCORE):
            extra[c]["stream_i"] = np.asarray(r[c]["stream_o"])
        r = run("b2", extra)
        for c in range(NCORE):
            extra[c]["stream_i"] = np.asarray(r[c]["stream_o"])
        r = run("b3", extra)
        out = np.concatenate([np.asarray(r[c]["out"]) for c in range(NCORE)], axis=0)
    return out.astype(np.float32)


if __name__ == "__main__":
    import time
    for name in ["prep", "b0", "b1", "b2", "b3"]:
        t0 = time.time()
        nc, T = build_program(LAUNCH_CFGS[name])
        n = sum(1 for _ in nc.all_instructions())
        print(f"{name}: {n} instructions, build {time.time()-t0:.1f}s")
